# version 31
# baseline (speedup 1.0000x reference)
import os
import numpy as np
import ml_dtypes
import concourse.bass as bass
import concourse.mybir as mybir
from concourse.bass_utils import run_bass_kernel_spmd

F32 = mybir.dt.float32
BF16 = mybir.dt.bfloat16
FP8 = mybir.dt.float8e5
I32 = mybir.dt.int32
AF = mybir.ActivationFunctionType
ALU = mybir.AluOpType

T = 2048
C = 1024
L = 512
QL = 512
NH = 16
HS = 64
N = 4096
NB = T // 512
M23 = 8388608.0
B127 = 1065353216.0
HALF_INVLN2 = float(np.float32(0.5 / np.log(2.0)))

_SIM = os.environ.get("BASS_SIM_FP64") == "1"

ENGS = ("pe", "act", "dve", "pool", "sp")
BLOCK_ATTR = {"pe": "tensor", "act": "scalar", "dve": "vector", "pool": "gpsimd", "sp": "sync"}


class Prog:

    def __init__(self, nc, es):
        self.nc = nc
        self.eng = {"pe": nc.tensor, "act": nc.scalar, "dve": nc.vector,
                    "pool": nc.gpsimd, "sp": nc.sync}
        self.semkeys = ["pe", "act", "dve", "pool", "sp"]
        for q in ("sp", "pool", "act"):
            for i in range(8):
                self.semkeys.append(f"d{q}{i}")
        self.sems = {k: es.enter_context(nc.semaphore("s_" + k)) for k in self.semkeys}
        self.cnt = {k: 0 for k in self.semkeys}
        self.dslot = {"sp": 0, "pool": 0, "act": 0}
        self.steps = {e: [] for e in ENGS}
        self.waited = {e: {} for e in ENGS}
        self.bufs = {}

    def buf(self, key):
        b = self.bufs.get(key)
        if b is None:
            b = {"w": None, "r": dict(getattr(self, "birth", {}))}
            self.bufs[key] = b
        return b

    def mark_free_barrier(self):
        ev = dict(getattr(self, "birth", {}))
        for b in self.bufs.values():
            evs = ([b["w"]] if b["w"] else []) + list(b["r"].items())
            for s, v in evs:
                if ev.get(s, 0) < v:
                    ev[s] = v
        self.birth = ev

    def alias(self, new, olds):
        b = self.buf(new)
        for o in olds:
            ob = self.bufs.get(o)
            if not ob:
                continue
            evs = ([ob["w"]] if ob["w"] else []) + list(ob["r"].items())
            for s, v in evs:
                if b["r"].get(s, 0) < v:
                    b["r"][s] = v

    def op(self, e, fn, reads=(), writes=(), queue=None):
        need = {}

        def merge(ev):
            if ev is None:
                return
            s, v = ev
            if need.get(s, 0) < v:
                need[s] = v

        for k in reads:
            merge(self.buf(k)["w"])
        for k in writes:
            b = self.buf(k)
            merge(b["w"])
            for s, v in b["r"].items():
                merge((s, v))
        if queue is not None:
            slot = self.dslot[queue]
            self.dslot[queue] = (slot + 1) % 8
            semkey = f"d{queue}{slot}"
            inc = 16
            if self.cnt[semkey] > 0:
                if need.get(semkey, 0) < self.cnt[semkey]:
                    need[semkey] = self.cnt[semkey]
        else:
            semkey = e
            inc = 1
        wt = self.waited[e]
        waits = []
        for s in sorted(need):
            v = need[s]
            if s == e and not _SIM:
                continue
            if wt.get(s, 0) < v:
                waits.append((self.sems[s], v))
                wt[s] = v
        semobj = self.sems[semkey]

        def step(eng, waits=waits, fn=fn, semobj=semobj, inc=inc):
            for so, v in waits:
                eng.wait_ge(so, v)
            fn(eng).then_inc(semobj, inc)

        self.steps[e].append(step)
        val = self.cnt[semkey] + inc
        self.cnt[semkey] = val
        for k in reads:
            b = self.buf(k)
            if b["r"].get(semkey, 0) < val:
                b["r"][semkey] = val
        for k in writes:
            b = self.buf(k)
            b["w"] = (semkey, val)
            b["r"] = {}

    def wait_only(self, e, keys):
        need = {}
        for k in keys:
            b = self.buf(k)
            if b["w"] is not None:
                s, v = b["w"]
                if need.get(s, 0) < v:
                    need[s] = v
        wt = self.waited[e]
        waits = []
        for s in sorted(need):
            v = need[s]
            if wt.get(s, 0) < v:
                waits.append((self.sems[s], v))
                wt[s] = v

        def step(eng, waits=waits):
            for so, v in waits:
                eng.wait_ge(so, v)

        self.steps[e].append(step)

    def finish(self, block):
        for e in ENGS:
            lst = self.steps[e]

            def replay(eng, lst=lst):
                for s_ in lst:
                    s_(eng)

            getattr(block, BLOCK_ATTR[e])(replay)


class PsumPool:
    def __init__(self, tiles):
        self.tiles = tiles
        self.i = 0

    def take(self):
        k, t_ = self.tiles[self.i % len(self.tiles)]
        self.i += 1
        return k, t_


DBG = False

def build_program():
    nc = bass.Bass("TRN2", target_bir_lowering=False)

    d_xT = nc.dram_tensor("d_xT", [C, T], F32, kind="ExternalInput").ap()
    d_maskT = nc.dram_tensor("d_maskT", [C, T], F32, kind="ExternalInput").ap()
    d_wdq = nc.dram_tensor("d_wdq", [QL, C], F32, kind="ExternalInput").ap()
    d_wdqT = nc.dram_tensor("d_wdqT", [C, QL], F32, kind="ExternalInput").ap()
    d_wuqT = nc.dram_tensor("d_wuqT", [QL, C], F32, kind="ExternalInput").ap()
    d_wuk = nc.dram_tensor("d_wuk", [C, L], F32, kind="ExternalInput").ap()
    d_wuv = nc.dram_tensor("d_wuv", [C, L], F32, kind="ExternalInput").ap()
    d_wdkvT = nc.dram_tensor("d_wdkvT", [C, L], F32, kind="ExternalInput").ap()
    d_woT = nc.dram_tensor("d_woT", [C, C], F32, kind="ExternalInput").ap()
    d_w1T = nc.dram_tensor("d_w1T", [C, N], F32, kind="ExternalInput").ap()
    d_w2T = nc.dram_tensor("d_w2T", [N, C], F32, kind="ExternalInput").ap()
    d_b1c = nc.dram_tensor("d_b1c", [128, 32], F32, kind="ExternalInput").ap()
    d_b2c = nc.dram_tensor("d_b2c", [128, 8], F32, kind="ExternalInput").ap()
    d_trix = nc.dram_tensor("d_trix", [128, 896], BF16, kind="ExternalInput").ap()
    d_outT = nc.dram_tensor("d_outT", [C, T], F32, kind="ExternalOutput").ap()
    d_w1q8 = nc.dram_tensor("d_w1q8", [8 * 128, N], FP8, kind="Internal").ap()
    d_w2q8 = nc.dram_tensor("d_w2q8", [32 * 128, C], FP8, kind="Internal").ap()
    dbg = {}
    if DBG:
        for nm, shp, dt in [("g_xnb", [128, 8 * T], BF16), ("g_xnq", [128, 8 * T], FP8),
                            ("g_qT", [128, 8 * T], BF16), ("g_ckvT", [128, 4 * T], BF16),
                            ("g_keff", [128, 8 * L], BF16), ("g_vsb", [128, 4 * C], BF16),
                            ("g_usb", [128, 16 * NH * 65], BF16),
                            ("g_qlat", [128, 4 * T], BF16), ("g_pt", [128, 2 * T], BF16),
                            ("g_yatt", [128, 8 * T], BF16), ("g_hq", [128, 32 * 512], FP8),
                            ("g_rbsb", [128, T], F32), ("g_mrsb", [128, T], F32),
                            ("g_rowM", [1, T], F32), ("g_rowV", [1, T], F32)]:
            dbg[nm] = nc.dram_tensor(nm, shp, dt, kind="ExternalOutput").ap()

    def dump(nm, sb_ap, keys):
        if DBG:
            p.op("sp", lambda eng: eng.dma_start(out=dbg[nm], in_=sb_ap),
                 reads=list(keys), writes=["dbg_" + nm], queue="sp")

    from contextlib import ExitStack
    es = ExitStack()
    p = Prog(nc, es)

    psA = es.enter_context(nc.psum_tensor("psA", [128, 2048], F32))

    class TileView:
        def __init__(self, h, c0, w=512):
            self.h = h
            self.c0 = c0
            self.w = w

        def __getitem__(self, idx):
            r, c = idx
            start = (c.start or 0) + self.c0
            stop = (c.stop if c.stop is not None else self.w) + self.c0
            return self.h[r, start:stop]

    psPbig = es.enter_context(nc.psum_tensor("psPbig", [128, 2048], F32))
    pool_tiles = [(f"psP{i}", TileView(psPbig, i * 512)) for i in range(4)]
    ppool = PsumPool(pool_tiles)
    psA_keys = [f"psA{i}" for i in range(4)]

    _live = {}

    def SBA(name, cols, dt, parts=128, side="left"):
        cm = nc.sbuf_tensor(name, [parts, cols], dt, side=side)
        h = cm.__enter__()
        _live[name] = cm
        return h

    def SBR(name, cols, dt, parts=128):
        return SBA(name, cols, dt, parts=parts, side="right")

    def SBFREE(*names, barrier=True):
        for nm in names:
            _live.pop(nm).__exit__(None, None, None)
        if barrier:
            p.mark_free_barrier()

    yatt = SBA("yatt", 8 * T, BF16)
    xn_q = SBA("xn_q", 8 * T, FP8)
    trix = SBA("trix_sb", 896, BF16)
    consts = SBA("consts", 16, F32)
    onesb = SBA("onesb", 4, BF16)
    ones1f = SBA("ones1f", 128, F32, parts=1)
    onesf = SBA("onesf", 4, F32)
    ones1b = SBA("ones1b", 64, BF16, parts=1)
    if not _SIM:
        rowb = SBA("rowb", T, BF16, parts=1)
    rowM = SBA("rowM", T, F32, parts=1)
    b1q = SBA("b1q", 32, F32)
    b2q = SBA("b2q", 8, F32)
    b1q8 = SBA("b1q8", 32, FP8)
    b2q8 = SBA("b2q8", 8, FP8)

    if _SIM:
        QW = 32
        qs = {}
        for n_ in ("sq", "l2", "av", "ev", "tmp", "bp", "bm", "r0", "dv", "c1", "c2",
                   "e1", "e2", "e1b", "mu", "me"):
            qs[n_] = SBA("qs_" + n_, QW, F32)
        import jax as _jax
        _jax.config.update("jax_enable_x64", True)
        _cpu = _jax.devices("cpu")[0]
        qs["tie_up"] = []
        qs["tie_even"] = []
        with _jax.default_device(_cpu):
            import jax.numpy as _jnp
            for s_ in range(-45, 20):
                es2v = float(_jnp.exp2(_jnp.float64(s_)))
                beh = []
                for k_ in (4, 5):
                    t_ = np.float64((k_ + 0.5) * (2.0 ** s_)) / np.float64(es2v)
                    beh.append(int(np.round(t_)) - k_)
                if beh == [0, 0]:
                    pass
                elif beh == [1, 1]:
                    qs["tie_up"].append(s_)
                else:
                    qs["tie_even"].append(s_)
        for n_ in ("ei", "ti", "bpi", "bmi"):
            qs[n_] = SBA("qs_" + n_, QW, I32)
    else:
        QW = 512
        qs = None

    for k in p.semkeys:
        nc.vector.sem_clear(p.sems[k])

    C_MAGIC = 6755399441055744.0
    V52 = 4503599627370497.0
    V29 = 536870913.0
    LN2_64 = float(np.log(np.float64(2.0)))

    def emit_quant_sim_blk(x_ap, out_ap, nconst, lo_clamp, reads, writes):
        W = x_ap.shape[1]
        assert W <= QW
        ks = "qscratch"
        sq, l2, av, ev, tmp = qs["sq"], qs["l2"], qs["av"], qs["ev"], qs["tmp"]
        bp, bm, r0, dv, c1, c2 = qs["bp"], qs["bm"], qs["r0"], qs["dv"], qs["c1"], qs["c2"]
        e1, e2, e1b, mu, me = qs["e1"], qs["e2"], qs["e1b"], qs["mu"], qs["me"]

        def V(fn):
            p.op("dve", fn, reads=[ks], writes=[ks])

        if nconst == 7:
            p.op("dve", lambda eng, x_ap=x_ap: eng.tensor_scalar(
                out=e1b[:, :W], in0=x_ap, scalar1=V29, scalar2=None, op0=ALU.mult),
                 reads=list(reads) + [ks], writes=[ks])
            p.op("dve", lambda eng, x_ap=x_ap: eng.scalar_tensor_tensor(
                out=bm[:, :W], in0=x_ap, scalar=-1.0, in1=e1b[:, :W],
                op0=ALU.mult, op1=ALU.add),
                 reads=list(reads) + [ks], writes=[ks])
            V(lambda eng: eng.tensor_tensor(out=e1b[:, :W], in0=e1b[:, :W], in1=bm[:, :W],
                                            op=ALU.subtract))
            x_ap = e1b[:, :W]
            reads = [ks]

        p.op("act", lambda eng, x_ap=x_ap: eng.activation(out=sq[:, :W], in_=x_ap,
                                                          func=AF.Square),
             reads=list(reads), writes=[ks])
        p.op("act", lambda eng: eng.activation(out=l2[:, :W], in_=sq[:, :W], func=AF.Ln,
                                               bias=consts[:, 0:1], scale=1.0),
             reads=[ks, "consts"], writes=[ks])
        V(lambda eng: eng.tensor_scalar(out=av[:, :W], in0=l2[:, :W], scalar1=HALF_INVLN2,
                                        scalar2=-0.5, op0=ALU.mult, op1=ALU.add))
        V(lambda eng: eng.tensor_scalar(out=ev[:, :W], in0=av[:, :W], scalar1=C_MAGIC,
                                        scalar2=None, op0=ALU.add))
        V(lambda eng: eng.tensor_scalar(out=ev[:, :W], in0=ev[:, :W], scalar1=C_MAGIC,
                                        scalar2=None, op0=ALU.subtract))
        V(lambda eng: eng.tensor_scalar(out=tmp[:, :W], in0=ev[:, :W], scalar1=lo_clamp,
                                        scalar2=-1.0, op0=ALU.max, op1=ALU.mult))
        V(lambda eng: eng.tensor_scalar(out=tmp[:, :W], in0=tmp[:, :W], scalar1=float(nconst),
                                        scalar2=None, op0=ALU.add))
        V(lambda eng: eng.tensor_scalar(out=bp[:, :W], in0=tmp[:, :W], scalar1=-LN2_64,
                                        scalar2=None, op0=ALU.mult))
        p.op("act", lambda eng: eng.activation(out=bp[:, :W], in_=bp[:, :W], func=AF.Exp),
             reads=[ks], writes=[ks])
        V(lambda eng: eng.tensor_scalar(out=e1[:, :W], in0=bp[:, :W], scalar1=V52,
                                        scalar2=None, op0=ALU.mult))
        V(lambda eng: eng.scalar_tensor_tensor(out=bm[:, :W], in0=bp[:, :W], scalar=-1.0,
                                               in1=e1[:, :W], op0=ALU.mult, op1=ALU.add))
        V(lambda eng: eng.tensor_tensor(out=e1[:, :W], in0=e1[:, :W], in1=bm[:, :W],
                                        op=ALU.subtract))
        V(lambda eng: eng.tensor_scalar(out=bp[:, :W], in0=tmp[:, :W], scalar1=LN2_64,
                                        scalar2=None, op0=ALU.mult))
        p.op("act", lambda eng: eng.activation(out=bp[:, :W], in_=bp[:, :W], func=AF.Exp),
             reads=[ks], writes=[ks])
        V(lambda eng: eng.tensor_scalar(out=e2[:, :W], in0=bp[:, :W], scalar1=V52,
                                        scalar2=None, op0=ALU.mult))
        V(lambda eng: eng.scalar_tensor_tensor(out=bm[:, :W], in0=bp[:, :W], scalar=-1.0,
                                               in1=e2[:, :W], op0=ALU.mult, op1=ALU.add))
        V(lambda eng: eng.tensor_tensor(out=e2[:, :W], in0=e2[:, :W], in1=bm[:, :W],
                                        op=ALU.subtract))
        p.op("dve", lambda eng, x_ap=x_ap: eng.tensor_tensor(out=tmp[:, :W], in0=x_ap,
                                                              in1=e2[:, :W], op=ALU.mult),
             reads=list(reads) + [ks], writes=[ks])
        V(lambda eng: eng.tensor_scalar(out=r0[:, :W], in0=tmp[:, :W], scalar1=C_MAGIC,
                                        scalar2=None, op0=ALU.add))
        V(lambda eng: eng.tensor_scalar(out=r0[:, :W], in0=r0[:, :W], scalar1=C_MAGIC,
                                        scalar2=None, op0=ALU.subtract))
        if nconst == 2:
            V(lambda eng: eng.tensor_scalar(out=av[:, :W], in0=ev[:, :W], scalar1=lo_clamp,
                                            scalar2=-1.0, op0=ALU.max, op1=ALU.mult))
            V(lambda eng: eng.tensor_scalar(out=av[:, :W], in0=av[:, :W],
                                            scalar1=float(nconst), scalar2=None, op0=ALU.add))
            V(lambda eng: eng.scalar_tensor_tensor(out=dv[:, :W], in0=r0[:, :W], scalar=-1.0,
                                                   in1=tmp[:, :W], op0=ALU.mult, op1=ALU.add))
            V(lambda eng: eng.tensor_scalar(out=c1[:, :W], in0=dv[:, :W], scalar1=0.5,
                                            scalar2=None, op0=ALU.is_equal))
            V(lambda eng: eng.tensor_scalar(out=c2[:, :W], in0=dv[:, :W], scalar1=-0.5,
                                            scalar2=None, op0=ALU.is_equal))
            V(lambda eng: eng.tensor_tensor(out=c2[:, :W], in0=c1[:, :W], in1=c2[:, :W],
                                            op=ALU.add))
            V(lambda eng: eng.tensor_scalar(out=c1[:, :W], in0=tmp[:, :W], scalar1=0.0,
                                            scalar2=None, op0=ALU.is_gt))
            V(lambda eng: eng.tensor_scalar(out=c1[:, :W], in0=c1[:, :W], scalar1=2.0,
                                            scalar2=-1.0, op0=ALU.mult, op1=ALU.add))
            first = True
            for s_ in qs["tie_up"]:
                if first:
                    V(lambda eng, s_=s_: eng.tensor_scalar(
                        out=mu[:, :W], in0=av[:, :W], scalar1=float(-s_), scalar2=None,
                        op0=ALU.is_equal))
                    first = False
                else:
                    V(lambda eng, s_=s_: eng.scalar_tensor_tensor(
                        out=mu[:, :W], in0=av[:, :W], scalar=float(-s_), in1=mu[:, :W],
                        op0=ALU.is_equal, op1=ALU.add))
            if first:
                V(lambda eng: eng.memset(mu[:, :W], 0.0))
            first = True
            for s_ in qs["tie_even"]:
                if first:
                    V(lambda eng, s_=s_: eng.tensor_scalar(
                        out=me[:, :W], in0=av[:, :W], scalar1=float(-s_), scalar2=None,
                        op0=ALU.is_equal))
                    first = False
                else:
                    V(lambda eng, s_=s_: eng.scalar_tensor_tensor(
                        out=me[:, :W], in0=av[:, :W], scalar=float(-s_), in1=me[:, :W],
                        op0=ALU.is_equal, op1=ALU.add))
            if first:
                V(lambda eng: eng.memset(me[:, :W], 0.0))
            V(lambda eng: eng.tensor_scalar(out=av[:, :W], in0=mu[:, :W], scalar1=-1.0,
                                            scalar2=1.0, op0=ALU.mult, op1=ALU.add))
            V(lambda eng: eng.tensor_tensor(out=av[:, :W], in0=av[:, :W], in1=me[:, :W],
                                            op=ALU.subtract))
            V(lambda eng: eng.scalar_tensor_tensor(out=me[:, :W], in0=c1[:, :W], scalar=0.5,
                                                   in1=dv[:, :W], op0=ALU.mult, op1=ALU.add))
            V(lambda eng: eng.scalar_tensor_tensor(out=e1b[:, :W], in0=c1[:, :W], scalar=-0.5,
                                                   in1=dv[:, :W], op0=ALU.mult, op1=ALU.add))
            V(lambda eng: eng.tensor_tensor(out=mu[:, :W], in0=mu[:, :W], in1=me[:, :W],
                                            op=ALU.mult))
            V(lambda eng: eng.tensor_tensor(out=av[:, :W], in0=av[:, :W], in1=e1b[:, :W],
                                            op=ALU.mult))
            V(lambda eng: eng.tensor_tensor(out=mu[:, :W], in0=mu[:, :W], in1=av[:, :W],
                                            op=ALU.add))
            V(lambda eng: eng.tensor_tensor(out=mu[:, :W], in0=mu[:, :W], in1=c2[:, :W],
                                            op=ALU.mult))
            V(lambda eng: eng.tensor_tensor(out=r0[:, :W], in0=r0[:, :W], in1=mu[:, :W],
                                            op=ALU.add))
        p.op("dve", lambda eng: eng.tensor_tensor(out=out_ap, in0=r0[:, :W], in1=e1[:, :W],
                                                  op=ALU.mult),
             reads=[ks], writes=list(writes))

    def emit_quant_sim(x_ap, out_ap, nconst, lo_clamp, reads, writes):
        W = x_ap.shape[1]
        for o in range(0, W, QW):
            w = min(QW, W - o)
            emit_quant_sim_blk(x_ap[:, o:o + w], out_ap[:, o:o + w], nconst, lo_clamp,
                               reads, writes)

    def quantize(x_ap, out_ap, kind, reads, writes):
        if not _SIM:
            p.op("dve", lambda eng: eng.tensor_copy(out=out_ap, in_=x_ap),
                 reads=list(reads), writes=list(writes))
        elif kind == "fq":
            emit_quant_sim(x_ap, out_ap, 2, -14.0, reads, writes)
        else:
            emit_quant_sim(x_ap, out_ap, 7, -50.0, reads, writes)

    p.op("dve", lambda eng: eng.memset(consts[:, 0:1], 1e-38), writes=["consts"])
    p.op("dve", lambda eng: eng.memset(consts[:, 1:2], 1e-5), writes=["consts"])
    p.op("dve", lambda eng: eng.memset(onesb[:, 0:1], 1.0), writes=["onesb"])
    p.op("dve", lambda eng: eng.memset(ones1f[:, :], 1.0), writes=["ones1f"])
    p.op("dve", lambda eng: eng.memset(onesf[:, 0:1], 1.0), writes=["onesf"])
    p.op("dve", lambda eng: eng.memset(ones1b[:, :], 1.0), writes=["ones1b"])
    p.op("sp", lambda eng: eng.dma_start(out=trix[:], in_=d_trix), writes=["trix"], queue="sp")
    p.op("sp", lambda eng: eng.dma_start(out=b1q[:, :], in_=d_b1c), writes=["b1f"], queue="sp")
    p.op("sp", lambda eng: eng.dma_start(out=b2q[:, :], in_=d_b2c), writes=["b2f"], queue="sp")
    quantize(b1q[:, :], b1q8[:, :], "fq", ["b1f"], ["b1q8"])
    quantize(b2q[:, :], b2q8[:, :], "fq", ["b2f"], ["b2q8"])
    p.op("dve", lambda eng: eng.tensor_copy(out=b1q[:, :], in_=b1q8[:, :]),
         reads=["b1q8"], writes=["b1q"])
    p.op("dve", lambda eng: eng.tensor_copy(out=b2q[:, :], in_=b2q8[:, :]),
         reads=["b2q8"], writes=["b2q"])

    keff = SBA("keff", 8 * L, BF16)
    vsb = SBR("vsb", 4 * C, BF16)
    wuqT_b = SBR("wuqT_b", 4096, BF16)
    wdkvT_b = SBR("wdkvT_b", 4096, BF16)
    wdqT_b = SBR("wdqT_b", 4096, BF16)
    stg = SBR("stg", 2 * 4096, F32)
    mt = SBR("mt", 8 * C, BF16)
    wdq_b = SBR("wdq_b", 4096, BF16)
    wuk_b = SBR("wuk_b", 4096, BF16)
    wuv_b = SBR("wuv_b", 4096, BF16)
    woT_b = SBR("woT_b", 8192, BF16)

    def load_cast(dst, dst_key, src_ap, cols, half):
        a = src_ap.shape[0] // 128
        w = cols // a
        sslice = stg[:, half * 4096: half * 4096 + cols]
        skey = f"stg{half}"
        p.op("sp", lambda eng: eng.dma_start(
            out=sslice.rearrange("p (a n) -> p a n", a=a),
            in_=src_ap.rearrange("(a p) n -> p a n", p=128)),
            writes=[skey], queue="sp")
        p.op("dve", lambda eng: eng.tensor_copy(out=dst[:, 0:cols], in_=sslice),
             reads=[skey], writes=[dst_key])

    load_cast(wdqT_b, "wdqT_b", d_wdqT, 4096, 0)
    load_cast(wuqT_b, "wuqT_b", d_wuqT, 4096, 1)
    load_cast(wdq_b, "wdq_b", d_wdq, 4096, 0)
    load_cast(wuk_b, "wuk_b", d_wuk, 4096, 1)
    load_cast(wuv_b, "wuv_b", d_wuv, 4096, 0)
    load_cast(wdkvT_b, "wdkvT_b", d_wdkvT, 4096, 1)
    for h_ in range(2):
        sslice = stg[:, h_ * 4096:(h_ + 1) * 4096]
        skey = f"stg{h_}"
        p.op("sp", lambda eng, s=sslice, h_=h_: eng.dma_start(
            out=s.rearrange("p (a n) -> p a n", n=1024),
            in_=d_woT.rearrange("(b a p) n -> b p a n", b=2, p=128)[h_]),
            writes=[skey], queue="sp")
        p.op("dve", lambda eng, s=sslice, h_=h_: eng.tensor_copy(
            out=woT_b[:, h_ * 4096:(h_ + 1) * 4096], in_=s),
            reads=[skey], writes=["woT_b"])

    def mm_block(dst_ap, dst_key, pairs, read_keys):
        pk, pt = ppool.take()
        nk = len(pairs)
        for i, (la, ra) in enumerate(pairs):
            p.op("pe", lambda eng, la=la, ra=ra, i=i, pt=pt, nk=nk: eng.matmul(
                pt[:, 0:ra.shape[1]], la, ra, start=(i == 0), stop=(i == nk - 1)),
                reads=list(read_keys), writes=[pk])
        p.op("act", lambda eng, pt=pt, o=dst_ap: eng.activation(
            out=o, in_=pt[:, 0:o.shape[1]], func=AF.Copy),
            reads=[pk], writes=[dst_key])

    for jc in range(8):
        for ib in range(2):
            pairs = [(wuqT_b[:, qc * 1024 + jc * 128: qc * 1024 + (jc + 1) * 128],
                      wdq_b[:, qc * 1024 + ib * 512: qc * 1024 + (ib + 1) * 512])
                     for qc in range(4)]
            mm_block(mt[:, jc * 1024 + ib * 512: jc * 1024 + (ib + 1) * 512], "mt",
                     pairs, ["wuqT_b", "wdq_b"])
    for ic in range(8):
        pairs = [(mt[:, jc * 1024 + ic * 128: jc * 1024 + (ic + 1) * 128],
                  wuk_b[:, jc * 512:(jc + 1) * 512]) for jc in range(8)]
        mm_block(keff[:, ic * 512:(ic + 1) * 512], "keff", pairs, ["mt", "wuk_b"])
    for lc in range(4):
        for cb in range(2):
            pairs = [(wuv_b[:, cc * 512 + lc * 128: cc * 512 + (lc + 1) * 128],
                      woT_b[:, cc * 1024 + cb * 512: cc * 1024 + (cb + 1) * 512])
                     for cc in range(8)]
            mm_block(vsb[:, lc * 1024 + cb * 512: lc * 1024 + (cb + 1) * 512], "vsb",
                     pairs, ["wuv_b", "woT_b"])
    SBFREE("woT_b", "wuv_b", "wuk_b", "wdq_b", "mt", "stg")

    xnb = SBR("xnb", 8 * T, BF16)
    xf = SBR("xf", (T if _SIM else 2 * T), F32)
    xnf = SBR("xnf", T, F32)
    rbsb = SBR("rbsb", T, F32)
    mrsb = SBR("mrsb", T, F32)
    rowV = SBR("rowV", T, F32, parts=1)
    rowT = SBR("rowT", T, F32, parts=1)
    for nk in ("xnb", "xf0", "xf1", "xbq", "x2q", "xnf", "rbsb", "mrsb", "lnvar", "lntmp"):
        p.alias(nk, ["mt", "wdq_b", "wuk_b", "wuv_b", "woT_b", "stg0", "stg1"])

    xT_r = d_xT.rearrange("(a p) t -> a p t", p=128)
    for cj in range(8):
        _sl = 0 if _SIM else (cj % 2)
        xf_sl = xf[:, _sl * T:(_sl + 1) * T]
        xkey = f"xf{_sl}"
        p.op("pool", lambda eng, cj=cj, s=xf_sl: eng.dma_start(out=s, in_=xT_r[cj]),
             writes=[xkey], queue="pool")
        p.op("act", lambda eng, s=xf_sl: eng.activation(out=xnf[:, :], in_=s, func=AF.Square),
             reads=[xkey], writes=["xsq"])
        for b in range(NB):
            p.op("pe", lambda eng, b=b, cj=cj, s=xf_sl: eng.matmul(
                psA[0:1, b * 512:(b + 1) * 512], onesf[:, 0:1],
                s[:, b * 512:(b + 1) * 512], start=(cj == 0), stop=(cj == 7)),
                reads=[xkey, "onesf"], writes=[f"psA{b}"])
            p.op("pe", lambda eng, b=b, cj=cj: eng.matmul(
                pool_tiles[b][1][0:1, :], onesf[:, 0:1],
                xnf[:, b * 512:(b + 1) * 512], start=(cj == 0), stop=(cj == 7)),
                reads=["xsq", "onesf"], writes=[pool_tiles[b][0]])
    p.op("dve", lambda eng: eng.tensor_scalar(out=rowM[:, :], in0=psA[0:1, :], scalar1=1.0 / C,
                                              scalar2=None, op0=ALU.mult),
         reads=psA_keys, writes=["lnmean"])
    for b in range(NB):
        p.op("dve", lambda eng, b=b: eng.tensor_scalar(
            out=rowV[:, b * 512:(b + 1) * 512], in0=pool_tiles[b][1][0:1, :],
            scalar1=1.0 / C, scalar2=None, op0=ALU.mult),
            reads=[pool_tiles[b][0]], writes=["lnvar"])
    p.op("act", lambda eng: eng.activation(out=rowT[:, :], in_=rowM[:, :], func=AF.Square),
         reads=["lnmean"], writes=["lntmp"])
    p.op("dve", lambda eng: eng.tensor_tensor(out=rowV[:, :], in0=rowV[:, :], in1=rowT[:, :],
                                              op=ALU.subtract),
         reads=["lnvar", "lntmp"], writes=["lnvar"])
    p.op("act", lambda eng: eng.activation(out=rowV[:, :], in_=rowV[:, :], func=AF.Ln,
                                           bias=consts[0:1, 1:2], scale=1.0),
         reads=["lnvar", "consts"], writes=["lnvar"])
    p.op("act", lambda eng: eng.activation(out=rowV[:, :], in_=rowV[:, :], func=AF.Exp,
                                           scale=-0.5),
         reads=["lnvar"], writes=["lnvar"])
    p.op("dve", lambda eng: eng.tensor_tensor(out=rowT[:, :], in0=rowM[:, :], in1=rowV[:, :],
                                              op=ALU.mult),
         reads=["lnmean", "lnvar"], writes=["lntmp"])
    for b in range(NB):
        pk, pt = ppool.take()
        p.op("pe", lambda eng, b=b, pt=pt: eng.matmul(
            pt[:, :], ones1f[:, :], rowV[:, b * 512:(b + 1) * 512], start=True, stop=True),
            reads=["lnvar", "ones1f"], writes=[pk])
        p.op("dve", lambda eng, b=b, pt=pt: eng.tensor_copy(
            out=rbsb[:, b * 512:(b + 1) * 512], in_=pt[:, :]),
            reads=[pk], writes=["rbsb"])
    for b in range(NB):
        pk2, pt2 = ppool.take()
        p.op("pe", lambda eng, b=b, pt=pt2: eng.matmul(
            pt[:, :], ones1f[:, :], rowT[:, b * 512:(b + 1) * 512], start=True, stop=True),
            reads=["lntmp", "ones1f"], writes=[pk2])
        p.op("dve", lambda eng, b=b, pt=pt2: eng.tensor_copy(
            out=mrsb[:, b * 512:(b + 1) * 512], in_=pt[:, :]),
            reads=[pk2], writes=["mrsb"])
    SBFREE("rowT", "rowV")

    for cj in range(8):
        _sl = 0 if _SIM else (cj % 2)
        xf_sl = xf[:, _sl * T:(_sl + 1) * T]
        xkey = f"xf{_sl}"
        p.op("pool", lambda eng, cj=cj, s=xf_sl: eng.dma_start(out=s, in_=xT_r[cj]),
             writes=[xkey], queue="pool")
        p.op("dve", lambda eng, s=xf_sl: eng.tensor_tensor(out=xnf[:, :], in0=s, in1=rbsb[:, :],
                                                           op=ALU.mult),
             reads=[xkey, "rbsb"], writes=["xnf"])
        p.op("dve", lambda eng: eng.tensor_tensor(out=xnf[:, :], in0=xnf[:, :], in1=mrsb[:, :],
                                                  op=ALU.subtract),
             reads=["xnf", "mrsb"], writes=["xnf"])
        p.op("act", lambda eng, cj=cj: eng.activation(out=xnb[:, cj * T:(cj + 1) * T],
                                                      in_=xnf[:, :], func=AF.Copy),
             reads=["xnf"], writes=["xnb"])
        quantize(xnf[:, :], xn_q[:, cj * T:(cj + 1) * T], "fq", ["xnf"], ["xn_q"])
    dump("g_xnb", xnb[:], ["xnb"])
    dump("g_xnq", xn_q[:], ["xn_q"])
    dump("g_rbsb", rbsb[:], ["rbsb"])
    dump("g_mrsb", mrsb[:], ["mrsb"])
    dump("g_rowM", rowM[:], ["lnmean"])
    SBFREE("mrsb", "rbsb", "xnf", "xf")

    q1b = SBR("q1b", 4 * T, BF16)
    p.alias("q1b", ["xf0", "xf1", "xbq", "x2q", "xnf", "rbsb", "mrsb"])
    for qs_ in range(4):
        for b in range(NB):
            pairs = [(wdqT_b[:, cc * 512 + qs_ * 128: cc * 512 + (qs_ + 1) * 128],
                      xnb[:, cc * T + b * 512: cc * T + (b + 1) * 512]) for cc in range(8)]
            mm_block(q1b[:, qs_ * T + b * 512: qs_ * T + (b + 1) * 512], "q1b",
                     pairs, ["wdqT_b", "xnb"])
    qT = SBA("qT", 8 * T, BF16)
    for cs in range(8):
        for b in range(NB):
            pairs = [(wuqT_b[:, qc * 1024 + cs * 128: qc * 1024 + (cs + 1) * 128],
                      q1b[:, qc * T + b * 512: qc * T + (b + 1) * 512]) for qc in range(4)]
            mm_block(qT[:, cs * T + b * 512: cs * T + (b + 1) * 512], "qT",
                     pairs, ["wuqT_b", "q1b"])
    dump("g_qT", qT[:], ["qT"])
    SBFREE("q1b")
    ckvT = SBA("ckvT", 4 * T, BF16)
    for ls in range(4):
        for b in range(NB):
            pairs = [(wdkvT_b[:, cc * 512 + ls * 128: cc * 512 + (ls + 1) * 128],
                      xnb[:, cc * T + b * 512: cc * T + (b + 1) * 512]) for cc in range(8)]
            mm_block(ckvT[:, ls * T + b * 512: ls * T + (b + 1) * 512], "ckvT",
                     pairs, ["wdkvT_b", "xnb"])
    dump("g_ckvT", ckvT[:], ["ckvT"])
    dump("g_keff", keff[:], ["keff"])
    dump("g_vsb", vsb[:], ["vsb"])
    SBFREE("xnb", "wdqT_b", "wdkvT_b", "wuqT_b")
    usb = SBA("usb", 16 * (NH * 65), BF16)
    for _kc in range(16):
        for _hh in range(NH):
            p.op("dve", lambda eng, _kc=_kc, _hh=_hh: eng.memset(
                usb[:, _kc * 1040 + _hh * 65 + 64: _kc * 1040 + _hh * 65 + 65], 1.0),
                writes=["usb_ones"])
    for kc in range(16):
        for cb in range(2):
            pk, pt = ppool.take()
            for lc in range(4):
                p.op("pe", lambda eng, kc=kc, cb=cb, lc=lc, pt=pt: eng.matmul(
                    pt[:, :], ckvT[:, lc * T + kc * 128: lc * T + (kc + 1) * 128],
                    vsb[:, lc * 1024 + cb * 512: lc * 1024 + (cb + 1) * 512],
                    start=(lc == 0), stop=(lc == 3)),
                    reads=["ckvT", "vsb"], writes=[pk])
            out_ap = usb[:, kc * 1040 + cb * 8 * 65: kc * 1040 + (cb * 8 + 8) * 65] \
                .rearrange("p (h o) -> p h o", h=8)[:, :, 0:64]
            p.op("act", lambda eng, pt=pt, o=out_ap: eng.activation(
                out=o, in_=pt[:, :].rearrange("p (h d) -> p h d", h=8), func=AF.Copy),
                reads=[pk], writes=["usb"])
    dump("g_usb", usb[:], ["usb", "usb_ones"])
    SBFREE("vsb")

    stg2 = SBR("stg2", 4096, F32)
    stg8 = SBR("stg8", 4096, FP8)
    for nk in ("stg2_0", "stg8_0"):
        p.alias(nk, ["q1b", "xnb", "wdqT_b", "wdkvT_b", "wuqT_b"])
    w1T_r = d_w1T.rearrange("(a p) n -> a p n", p=128)
    w1q8_r = d_w1q8.rearrange("(a p) n -> a p n", p=128)
    w2T_r = d_w2T.rearrange("(a p) n -> a p n", p=128)
    w2q8_r = d_w2q8.rearrange("(a p) n -> a p n", p=128)
    wf_steps = []

    def _wf_step(mat_r, q8_r, a, cols):
        def go():
            sslice = stg2[:, 0:cols]
            skey = "stg2_0"
            s8 = stg8[:, 0:cols]
            s8key = "stg8_0"
            dkey = (f"w1q8dram_{a}" if q8_r is w1q8_r else f"w2q8dram_{a}")
            p.op("sp", lambda eng, a=a, s=sslice: eng.dma_start(out=s, in_=mat_r[a]),
                 writes=[skey], queue="sp")
            quantize(sslice, s8, "fq", [skey], [s8key])
            p.op("pool", lambda eng, a=a, s8=s8: eng.dma_start(out=q8_r[a], in_=s8),
                 reads=[s8key], writes=[dkey], queue="pool")
        return go

    for a in range(8):
        wf_steps.append(_wf_step(w1T_r, w1q8_r, a, 4096))
    for a in range(32):
        wf_steps.append(_wf_step(w2T_r, w2q8_r, a, 1024))

    qlat = SBA("qlat", 4 * T, BF16)
    ptb = SBA("ptb", 2 * T, BF16)
    yae = SBA("yae", 1024, F32)
    ynb = SBA("ynb", T, BF16)
    ss32 = SBA("ss32", 32, F32, parts=32)
    rrow = SBA("rrow", 1024, F32, parts=1)
    for nk in ("qlat", "pt0_0", "pt0_1", "pt0_2", "pt0_3", "pt1_0", "pt1_1", "pt1_2",
               "pt1_3", "yae0", "yae1", "ss32", "rrow"):
        p.alias(nk, ["stg2_0", "stg2_1", "stg8_0", "stg8_1", "vsb"])

    def emit_qlat(h):
        b0 = (h % 2) * 64
        hc = h // 2
        for b in range(NB):
            for lc in range(4):
                pk, pt = ppool.take()
                p.op("pe", lambda eng, lc=lc, b=b, pt=pt, b0=b0, hc=hc: eng.matmul(
                    pt[:, :],
                    keff[b0:b0 + 64, hc * 512 + lc * 128: hc * 512 + (lc + 1) * 128],
                    qT[b0:b0 + 64, hc * T + b * 512: hc * T + (b + 1) * 512],
                    start=True, stop=True),
                    reads=["keff", "qT"], writes=[pk])
                p.op("act", lambda eng, lc=lc, b=b, pt=pt: eng.activation(
                    out=qlat[:, lc * T + b * 512: lc * T + (b + 1) * 512], in_=pt[:, :],
                    func=AF.Copy),
                    reads=[pk], writes=[f"qlat_{b}"])

    def emit_qk(h, kt):
        ptile = ptb[:, (kt % 2) * T:(kt % 2 + 1) * T]
        ptkey = f"pt{kt % 2}"
        b_lo = kt // 4
        for b in range(b_lo, NB):
            pk, pt = f"psP{b}", pool_tiles[b][1]
            for lc in range(4):
                p.op("pe", lambda eng, kt=kt, b=b, lc=lc, pt=pt: eng.matmul(
                    pt[:, :], ckvT[:, lc * T + kt * 128: lc * T + (kt + 1) * 128],
                    qlat[:, lc * T + b * 512: lc * T + (b + 1) * 512],
                    start=(lc == 0), stop=(lc == 3)),
                    reads=["ckvT", f"qlat_{b}"], writes=[pk])
        p.op("act", lambda eng, b_lo=b_lo, ptile=ptile: eng.activation(
            out=ptile[:, b_lo * 512:NB * 512], in_=psPbig[:, b_lo * 512:NB * 512],
            func=AF.Exp, scale=0.125),
            reads=[f"psP{b}" for b in range(b_lo, NB)],
            writes=[ptkey + f"_{b}" for b in range(b_lo, NB)])
        o = 128 * (kt % 4)
        p.op("dve", lambda eng, b=b_lo, ptile=ptile, o=o: eng.tensor_tensor(
            out=ptile[:, b * 512:(b + 1) * 512], in0=ptile[:, b * 512:(b + 1) * 512],
            in1=trix[:, 384 - o: 896 - o], op=ALU.mult),
            reads=[ptkey + f"_{b_lo}", "trix"], writes=[ptkey + f"_{b_lo}"])

    def emit_yatt(h, kt):
        ptile = ptb[:, (kt % 2) * T:(kt % 2 + 1) * T]
        ptkey = f"pt{kt % 2}"
        b_lo = kt // 4
        for b in range(b_lo, NB):
            p.op("pe", lambda eng, kt=kt, b=b, h=h, ptile=ptile: eng.matmul(
                psA[0:65, b * 512:(b + 1) * 512],
                usb[:, kt * 1040 + h * 65: kt * 1040 + (h + 1) * 65],
                ptile[:, b * 512:(b + 1) * 512],
                start=(kt == 0), stop=(kt == min(15, 4 * b + 3))),
                reads=[ptkey + f"_{b}", "usb", "usb_ones"], writes=[f"psA{b}"])

    def emit_epi_block(h, b):
        if h == 0 and b == 0:
            for bb in range(NB):
                p.alias(f"sumsrow_{bb}", ["lnmean"])
        p.op("act", lambda eng, b=b: eng.activation(
            out=rowM[:, b * 512:(b + 1) * 512], in_=psA[64:65, b * 512:(b + 1) * 512],
            func=AF.Copy),
            reads=[f"psA{b}"], writes=[f"sumsrow_{b}"])
        ya_sl = yae[:, (b % 2) * 512:(b % 2 + 1) * 512]
        p.op("act", lambda eng, b=b, ya=ya_sl: eng.activation(
            out=ya[0:64, :], in_=psA[0:64, b * 512:(b + 1) * 512], func=AF.Copy),
            reads=[f"psA{b}"], writes=[f"yae{b % 2}"])

    def emit_epi_recip(h, pair):
        blo = 2 * pair
        p.op("pool", lambda eng, blo=blo: eng.dma_start(
            out=ss32[:, :], in_=rowM[0:1, blo * 512:(blo + 2) * 512]),
            reads=[f"sumsrow_{blo}", f"sumsrow_{blo + 1}"], writes=["ss32"], queue="pool")
        p.op("dve", lambda eng: eng.reciprocal(out=ss32[:, :], in_=ss32[:, :]),
             reads=["ss32"], writes=["ss32"])
        p.op("pool", lambda eng: eng.dma_start(
            out=rrow[0:1, :], in_=ss32[:, :]),
            reads=["ss32"], writes=["rrow"], queue="pool")

    def emit_epi_norm(h, pair):
        b0 = (h % 2) * 64
        hc = h // 2
        for b in (2 * pair, 2 * pair + 1):
            pk, pt = ppool.take()
            p.op("pe", lambda eng, b=b, pt=pt: eng.matmul(
                pt[0:64, :], ones1f[:, 0:64], rrow[:, (b % 2) * 512:(b % 2 + 1) * 512],
                start=True, stop=True),
                reads=["rrow", "ones1f"], writes=[pk])
            ya_sl = yae[:, (b % 2) * 512:(b % 2 + 1) * 512]
            if b0 == 0:
                p.op("dve", lambda eng, b=b, pt=pt, ya=ya_sl, hc=hc: eng.tensor_tensor(
                    out=yatt[0:64, hc * T + b * 512: hc * T + (b + 1) * 512],
                    in0=ya[0:64, :], in1=pt[0:64, :], op=ALU.mult),
                    reads=[f"yae{b % 2}", pk], writes=["yatt"])
            else:
                p.op("dve", lambda eng, b=b, pt=pt, ya=ya_sl: eng.tensor_tensor(
                    out=ynb[0:64, b * 512:(b + 1) * 512],
                    in0=ya[0:64, :], in1=pt[0:64, :], op=ALU.mult),
                    reads=[f"yae{b % 2}", pk], writes=["ynb"])
        if b0 == 64 and pair == 1:
            p.op("pool", lambda eng, hc=hc: eng.dma_start(
                out=yatt[64:128, hc * T:(hc + 1) * T], in_=ynb[0:64, :]),
                reads=["ynb"], writes=["yatt"], queue="pool")

    wfi = 0
    tail_prev = None
    for h in range(NH):
        emit_qlat(h)
        if h == 0:
            dump("g_qlat", qlat[:], [f"qlat_{b}" for b in range(NB)])
        prev = None
        for kt in range(16):
            emit_qk(h, kt)
            if h == 0 and kt == 1:
                dump("g_pt", ptb[:], ["pt0_0", "pt0_1", "pt0_2", "pt0_3",
                                      "pt1_0", "pt1_1", "pt1_2", "pt1_3"])
            if kt == 1 and tail_prev is not None:
                tail_prev()
                tail_prev = None
            if prev is not None:
                emit_yatt(h, prev)
                if prev % 4 == 3 and prev < 15:
                    emit_epi_block(h, prev // 4)
            if kt == 9:
                emit_epi_recip(h, 0)
            if kt == 11:
                emit_epi_norm(h, 0)
            prev = kt
        emit_yatt(h, prev)
        emit_epi_block(h, 3)
        emit_epi_recip(h, 1)
        tail_prev = (lambda h=h: emit_epi_norm(h, 1))
        nsteps = (len(wf_steps) * min(h + 1, NH - 2)) // (NH - 2) - wfi
        for _ in range(nsteps):
            wf_steps[wfi]()
            wfi += 1
    tail_prev()
    assert wfi == len(wf_steps)
    dump("g_yatt", yatt[:], ["yatt"])
    SBFREE("stg8", "stg2", barrier=False)

    SBFREE("rrow", "ss32", "ynb", "yae", "ptb", "qlat", "usb", "ckvT", "qT", "keff",
           barrier=False)
    w1q = SBA("w1q", 8 * N, FP8)
    w2q = SBA("w2q", 32 * C, FP8)
    hq = SBA("hq", 32 * 512, FP8)
    hbf = SBA("hbf", 2 * 512, BF16)
    mstg = SBA("mstg", 4096, F32)
    xstg = SBA("xstg", 4096, F32)
    ostg = SBA("ostg", 4096, F32)
    otmp = SBA("otmp", 512, F32)
    y2bf = SBA("y2bf", 512, BF16)
    if _SIM:
        h1f = SBA("h1f", 512, F32)
        y2f = SBA("y2f", 512, F32)
    qlat_keys = [f"qlat_{b}" for b in range(NB)]
    pt_keys = ["pt0_0", "pt0_1", "pt0_2", "pt0_3", "pt1_0", "pt1_1", "pt1_2", "pt1_3"]
    usb_keys = ["usb", "usb_ones"]
    old_keys = ["xnb", "xf0", "xf1", "xnf", "xsq", "rbsb", "mrsb", "lnvar", "lntmp",
                "q1b", "vsb", "stg0", "stg1", "stg2_0", "stg8_0"]
    for a in range(8):
        p.alias(f"w1q_{a}", ["keff", "qT"])
    for a in range(32):
        ks = (["qT"] if a <= 8 else []) + (["ckvT"] if 7 <= a <= 24 else []) \
            + (usb_keys if a >= 23 else [])
        p.alias(f"w2q_{a}", ks)
    p.alias("hq", usb_keys)
    p.alias("hbf0", usb_keys)
    p.alias("hbf1", usb_keys)
    p.alias("mstg", usb_keys + qlat_keys)
    p.alias("xstg", qlat_keys + pt_keys + ["yae0", "yae1"])
    for cs in range(8):
        p.alias(f"ostg{cs}", ["yae0", "yae1", "ynb", "ss32", "rrow"] + old_keys)
    for nk in ("otmp", "y2bf", "h1f", "y2f"):
        p.alias(nk, ["ynb", "ss32", "rrow"] + old_keys)

    for a in range(8):
        p.op("sp", lambda eng, a=a: eng.dma_start(
            out=w1q[:, a * N:(a + 1) * N], in_=w1q8_r[a]),
            reads=[f"w1q8dram_{a}"], writes=[f"w1q_{a}"], queue="sp")
    for a in range(32):
        q = "sp" if a % 2 == 0 else "pool"
        p.op(q, lambda eng, a=a: eng.dma_start(
            out=w2q[:, a * C:(a + 1) * C], in_=w2q8_r[a]),
            reads=[f"w2q8dram_{a}"], writes=[f"w2q_{a}"], queue=q)

    mask_r = d_maskT.rearrange("(a p) (q t) -> q p a t", p=128, t=512)
    x_r2 = d_xT.rearrange("(a p) (q t) -> q p a t", p=128, t=512)
    out_r = d_outT.rearrange("(a p) (q t) -> q p a t", p=128, t=512)

    for ti in range(NB):
        p.op("sp", lambda eng, ti=ti: eng.dma_start(
            out=mstg[:].rearrange("p (a t) -> p a t", t=512), in_=mask_r[ti]),
            writes=["mstg"], queue="sp")
        p.op("pool", lambda eng, ti=ti: eng.dma_start(
            out=xstg[:].rearrange("p (a t) -> p a t", t=512), in_=x_r2[ti]),
            writes=["xstg"], queue="pool")
        for nj in range(32):
            pk, pt = ppool.take()
            for cc in range(8):
                p.op("pe", lambda eng, nj=nj, cc=cc, ti=ti, pt=pt: eng.matmul(
                    pt[:, :], w1q[:, cc * N + nj * 128: cc * N + (nj + 1) * 128],
                    xn_q[:, cc * T + ti * 512: cc * T + (ti + 1) * 512],
                    start=(cc == 0), stop=(cc == 7)),
                    reads=[f"w1q_{cc}", "xn_q"], writes=[pk])
            hbf_sl = hbf[:, (nj % 2) * 512:(nj % 2 + 1) * 512]
            hbkey = f"hbf{nj % 2}"
            if not _SIM:
                p.op("act", lambda eng, nj=nj, pt=pt, o=hbf_sl: eng.activation(
                    out=o, in_=pt[:, :], func=AF.Relu, bias=b1q[:, nj:nj + 1], scale=1.0),
                    reads=[pk, "b1q"], writes=[hbkey])
                p.op("dve", lambda eng, nj=nj, o=hbf_sl: eng.tensor_copy(
                    out=hq[:, nj * 512:(nj + 1) * 512], in_=o),
                    reads=[hbkey], writes=["hq"])
            else:
                p.op("act", lambda eng, nj=nj, pt=pt: eng.activation(
                    out=h1f[:, :], in_=pt[:, :], func=AF.Relu, bias=b1q[:, nj:nj + 1],
                    scale=1.0),
                    reads=[pk, "b1q"], writes=["h1f"])
                emit_quant_sim(h1f[:, :], hbf_sl, 7, -50.0, ["h1f"], [hbkey])
                emit_quant_sim(hbf_sl, hq[:, nj * 512:(nj + 1) * 512], 2, -14.0,
                               [hbkey], ["hq"])
        if ti == 3:
            dump("g_hq", hq[:], ["hq"])
        for cs in range(8):
            pk, pt = ppool.take()
            for nj in range(32):
                p.op("pe", lambda eng, cs=cs, nj=nj, pt=pt: eng.matmul(
                    pt[:, :], w2q[:, nj * C + cs * 128: nj * C + (cs + 1) * 128],
                    hq[:, nj * 512:(nj + 1) * 512],
                    start=(nj == 0), stop=(nj == 31)),
                    reads=[f"w2q_{nj}", "hq"], writes=[pk])
            if not _SIM:
                p.op("dve", lambda eng, cs=cs, pt=pt: eng.tensor_scalar(
                    out=y2bf[:, :], in0=pt[:, :], scalar1=b2q[:, cs:cs + 1], scalar2=None,
                    op0=ALU.add),
                    reads=[pk, "b2q"], writes=["y2bf"])
            else:
                p.op("dve", lambda eng, cs=cs, pt=pt: eng.tensor_scalar(
                    out=y2f[:, :], in0=pt[:, :], scalar1=b2q[:, cs:cs + 1], scalar2=None,
                    op0=ALU.add),
                    reads=[pk, "b2q"], writes=["y2f"])
                emit_quant_sim(y2f[:, :], y2bf[:, :], 7, -50.0, ["y2f"], ["y2bf"])
            p.op("dve", lambda eng, cs=cs: eng.tensor_tensor(
                out=otmp[:, :], in0=y2bf[:, :], in1=mstg[:, cs * 512:(cs + 1) * 512],
                op=ALU.mult),
                reads=["y2bf", "mstg"], writes=["otmp"])
            p.op("dve", lambda eng, cs=cs, ti=ti: eng.tensor_tensor(
                out=otmp[:, :], in0=otmp[:, :],
                in1=yatt[:, cs * T + ti * 512: cs * T + (ti + 1) * 512], op=ALU.add),
                reads=["otmp", "yatt"], writes=["otmp"])
            p.op("dve", lambda eng, cs=cs: eng.tensor_tensor(
                out=ostg[:, cs * 512:(cs + 1) * 512], in0=otmp[:, :],
                in1=xstg[:, cs * 512:(cs + 1) * 512], op=ALU.add),
                reads=["otmp", "xstg"], writes=[f"ostg{cs}"])
        p.op("pool", lambda eng, ti=ti: eng.dma_start(
            out=out_r[ti], in_=ostg[:].rearrange("p (a t) -> p a t", t=512)),
            reads=[f"ostg{cs_}" for cs_ in range(8)], writes=["outdram"], queue="pool")

    p.wait_only("pool", ["outdram"])
    p.wait_only("sp", ["outdram"])

    with nc.Block() as block:
        p.finish(block)
    es.close()
    return nc


def _host_layout(inputs):
    x = np.asarray(inputs["L0_attn_x"], dtype=np.float32)
    wdq = np.asarray(inputs["L0_attn_W_dq"], dtype=np.float32)
    wuq = np.asarray(inputs["L0_attn_W_uq"], dtype=np.float32)
    wdkv = np.asarray(inputs["L0_attn_W_dkv"], dtype=np.float32)
    wuk = np.asarray(inputs["L0_attn_W_uk"], dtype=np.float32)
    wuv = np.asarray(inputs["L0_attn_W_uv"], dtype=np.float32)
    wo = np.asarray(inputs["L0_attn_W_o"], dtype=np.float32)
    w1 = np.asarray(inputs["L0_ffn_W1"], dtype=np.float32)
    b1 = np.asarray(inputs["L0_ffn_b1"], dtype=np.float32)
    w2 = np.asarray(inputs["L0_ffn_W2"], dtype=np.float32)
    b2 = np.asarray(inputs["L0_ffn_b2"], dtype=np.float32)

    import jax
    cpu = jax.devices("cpu")[0]
    if _SIM:
        jax.config.update("jax_enable_x64", True)
    with jax.default_device(cpu):
        k42 = jax.random.key(42, impl="threefry2x32")
        keep = np.asarray(jax.random.bernoulli(k42, 0.8, (8, T, C)))
    maskv = np.where(keep, np.float32(1.25), np.float32(0.0)).astype(np.float32)

    kk = np.arange(128)[:, None]
    mm = np.arange(896)[None, :]
    trix = (mm >= kk + 384).astype(ml_dtypes.bfloat16)

    com = {
        "d_wdq": np.ascontiguousarray(wdq),
        "d_wdqT": np.ascontiguousarray(wdq.T),
        "d_wuqT": np.ascontiguousarray(wuq.T),
        "d_wuk": np.ascontiguousarray(wuk),
        "d_wuv": np.ascontiguousarray(wuv),
        "d_wdkvT": np.ascontiguousarray(wdkv.T),
        "d_woT": np.ascontiguousarray(wo.T),
        "d_w1T": np.ascontiguousarray(w1.T),
        "d_w2T": np.ascontiguousarray(w2.T),
        "d_b1c": np.ascontiguousarray(b1.reshape(32, 128).T),
        "d_b2c": np.ascontiguousarray(b2.reshape(8, 128).T),
        "d_trix": trix,
    }
    in_maps = []
    for core in range(8):
        m = dict(com)
        m["d_xT"] = np.ascontiguousarray(x[core].T)
        m["d_maskT"] = np.ascontiguousarray(maskv[core].T)
        in_maps.append(m)
    return in_maps


def kernel(**inputs):
    in_maps = _host_layout(inputs)
    nc = build_program()
    res = run_bass_kernel_spmd(nc, in_maps, core_ids=list(range(8)))
    out = np.stack([np.asarray(res.results[i]["d_outT"]) for i in range(8)], axis=0)
    return np.ascontiguousarray(out.transpose(0, 2, 1))


# version 32
# speedup vs baseline: 1.0587x; 1.0587x over previous
import os
import numpy as np
import ml_dtypes
import concourse.bass as bass
import concourse.mybir as mybir
from concourse.bass_utils import run_bass_kernel_spmd

F32 = mybir.dt.float32
BF16 = mybir.dt.bfloat16
FP8 = mybir.dt.float8e5
I32 = mybir.dt.int32
AF = mybir.ActivationFunctionType
ALU = mybir.AluOpType

T = 2048
C = 1024
L = 512
QL = 512
NH = 16
HS = 64
N = 4096
NB = T // 512
M23 = 8388608.0
B127 = 1065353216.0
HALF_INVLN2 = float(np.float32(0.5 / np.log(2.0)))

_SIM = os.environ.get("BASS_SIM_FP64") == "1"

ENGS = ("pe", "act", "dve", "pool", "sp")
BLOCK_ATTR = {"pe": "tensor", "act": "scalar", "dve": "vector", "pool": "gpsimd", "sp": "sync"}


class Prog:

    def __init__(self, nc, es):
        self.nc = nc
        self.eng = {"pe": nc.tensor, "act": nc.scalar, "dve": nc.vector,
                    "pool": nc.gpsimd, "sp": nc.sync}
        self.semkeys = ["pe", "act", "dve", "pool", "sp"]
        for q in ("sp", "pool", "act"):
            for i in range(8):
                self.semkeys.append(f"d{q}{i}")
        self.sems = {k: es.enter_context(nc.semaphore("s_" + k)) for k in self.semkeys}
        self.cnt = {k: 0 for k in self.semkeys}
        self.dslot = {"sp": 0, "pool": 0, "act": 0}
        self.steps = {e: [] for e in ENGS}
        self.waited = {e: {} for e in ENGS}
        self.bufs = {}

    def buf(self, key):
        b = self.bufs.get(key)
        if b is None:
            b = {"w": None, "r": dict(getattr(self, "birth", {}))}
            self.bufs[key] = b
        return b

    def mark_free_barrier(self):
        ev = dict(getattr(self, "birth", {}))
        for b in self.bufs.values():
            evs = ([b["w"]] if b["w"] else []) + list(b["r"].items())
            for s, v in evs:
                if ev.get(s, 0) < v:
                    ev[s] = v
        self.birth = ev

    def alias(self, new, olds):
        b = self.buf(new)
        for o in olds:
            ob = self.bufs.get(o)
            if not ob:
                continue
            evs = ([ob["w"]] if ob["w"] else []) + list(ob["r"].items())
            for s, v in evs:
                if b["r"].get(s, 0) < v:
                    b["r"][s] = v

    def op(self, e, fn, reads=(), writes=(), queue=None):
        need = {}

        def merge(ev):
            if ev is None:
                return
            s, v = ev
            if need.get(s, 0) < v:
                need[s] = v

        for k in reads:
            merge(self.buf(k)["w"])
        for k in writes:
            b = self.buf(k)
            merge(b["w"])
            for s, v in b["r"].items():
                merge((s, v))
        if queue is not None:
            slot = self.dslot[queue]
            self.dslot[queue] = (slot + 1) % 8
            semkey = f"d{queue}{slot}"
            inc = 16
            if self.cnt[semkey] > 0:
                if need.get(semkey, 0) < self.cnt[semkey]:
                    need[semkey] = self.cnt[semkey]
        else:
            semkey = e
            inc = 1
        wt = self.waited[e]
        waits = []
        for s in sorted(need):
            v = need[s]
            if s == e and not _SIM:
                continue
            if wt.get(s, 0) < v:
                waits.append((self.sems[s], v))
                wt[s] = v
        semobj = self.sems[semkey]

        def step(eng, waits=waits, fn=fn, semobj=semobj, inc=inc):
            for so, v in waits:
                eng.wait_ge(so, v)
            fn(eng).then_inc(semobj, inc)

        self.steps[e].append(step)
        val = self.cnt[semkey] + inc
        self.cnt[semkey] = val
        for k in reads:
            b = self.buf(k)
            if b["r"].get(semkey, 0) < val:
                b["r"][semkey] = val
        for k in writes:
            b = self.buf(k)
            b["w"] = (semkey, val)
            b["r"] = {}

    def wait_only(self, e, keys):
        need = {}
        for k in keys:
            b = self.buf(k)
            if b["w"] is not None:
                s, v = b["w"]
                if need.get(s, 0) < v:
                    need[s] = v
        wt = self.waited[e]
        waits = []
        for s in sorted(need):
            v = need[s]
            if wt.get(s, 0) < v:
                waits.append((self.sems[s], v))
                wt[s] = v

        def step(eng, waits=waits):
            for so, v in waits:
                eng.wait_ge(so, v)

        self.steps[e].append(step)

    def finish(self, block):
        for e in ENGS:
            lst = self.steps[e]

            def replay(eng, lst=lst):
                for s_ in lst:
                    s_(eng)

            getattr(block, BLOCK_ATTR[e])(replay)


class PsumPool:
    def __init__(self, tiles):
        self.tiles = tiles
        self.i = 0

    def take(self):
        k, t_ = self.tiles[self.i % len(self.tiles)]
        self.i += 1
        return k, t_


DBG = False

def build_program():
    nc = bass.Bass("TRN2", target_bir_lowering=False)

    d_xT = nc.dram_tensor("d_xT", [C, T], F32, kind="ExternalInput").ap()
    d_maskT = nc.dram_tensor("d_maskT", [C, T], F32, kind="ExternalInput").ap()
    d_wdq = nc.dram_tensor("d_wdq", [QL, C], F32, kind="ExternalInput").ap()
    d_wdqT = nc.dram_tensor("d_wdqT", [C, QL], F32, kind="ExternalInput").ap()
    d_wuqT = nc.dram_tensor("d_wuqT", [QL, C], F32, kind="ExternalInput").ap()
    d_wuk = nc.dram_tensor("d_wuk", [C, L], F32, kind="ExternalInput").ap()
    d_wuv = nc.dram_tensor("d_wuv", [C, L], F32, kind="ExternalInput").ap()
    d_wdkvT = nc.dram_tensor("d_wdkvT", [C, L], F32, kind="ExternalInput").ap()
    d_woT = nc.dram_tensor("d_woT", [C, C], F32, kind="ExternalInput").ap()
    d_w1T = nc.dram_tensor("d_w1T", [C, N], F32, kind="ExternalInput").ap()
    d_w2T = nc.dram_tensor("d_w2T", [N, C], F32, kind="ExternalInput").ap()
    d_b1c = nc.dram_tensor("d_b1c", [128, 32], F32, kind="ExternalInput").ap()
    d_b2c = nc.dram_tensor("d_b2c", [128, 8], F32, kind="ExternalInput").ap()
    d_trix = nc.dram_tensor("d_trix", [128, 896], BF16, kind="ExternalInput").ap()
    d_outT = nc.dram_tensor("d_outT", [C, T], F32, kind="ExternalOutput").ap()
    d_w1q8 = nc.dram_tensor("d_w1q8", [8 * 128, N], FP8, kind="Internal").ap()
    d_w2q8 = nc.dram_tensor("d_w2q8", [32 * 128, C], FP8, kind="Internal").ap()
    dbg = {}
    if DBG:
        for nm, shp, dt in [("g_xnb", [128, 8 * T], BF16), ("g_xnq", [128, 8 * T], FP8),
                            ("g_qT", [128, 8 * T], BF16), ("g_ckvT", [128, 4 * T], BF16),
                            ("g_keff", [128, 8 * L], BF16), ("g_vsb", [128, 4 * C], BF16),
                            ("g_usb", [128, 16 * NH * 65], BF16),
                            ("g_qlat", [128, 4 * T], BF16), ("g_pt", [128, 2 * T], BF16),
                            ("g_yatt", [128, 8 * T], BF16), ("g_hq", [128, 32 * 512], FP8),
                            ("g_rbsb", [128, T], F32), ("g_mrsb", [128, T], F32),
                            ("g_rowM", [1, T], F32), ("g_rowV", [1, T], F32)]:
            dbg[nm] = nc.dram_tensor(nm, shp, dt, kind="ExternalOutput").ap()

    def dump(nm, sb_ap, keys):
        if DBG:
            p.op("sp", lambda eng: eng.dma_start(out=dbg[nm], in_=sb_ap),
                 reads=list(keys), writes=["dbg_" + nm], queue="sp")

    from contextlib import ExitStack
    es = ExitStack()
    p = Prog(nc, es)

    psA = es.enter_context(nc.psum_tensor("psA", [128, 2048], F32))
    pool_tiles = [(f"psP{i}", es.enter_context(nc.psum_tensor(f"psP{i}", [128, 512], F32)))
                  for i in range(4)]
    ppool = PsumPool(pool_tiles)
    psA_keys = [f"psA{i}" for i in range(4)]

    _live = {}

    def SBA(name, cols, dt, parts=128, side="left"):
        cm = nc.sbuf_tensor(name, [parts, cols], dt, side=side)
        h = cm.__enter__()
        _live[name] = cm
        return h

    def SBR(name, cols, dt, parts=128):
        return SBA(name, cols, dt, parts=parts, side="right")

    def SBFREE(*names, barrier=True):
        for nm in names:
            _live.pop(nm).__exit__(None, None, None)
        if barrier:
            p.mark_free_barrier()

    yatt = SBA("yatt", 8 * T, BF16)
    xn_q = SBA("xn_q", 8 * T, FP8)
    trix = SBA("trix_sb", 896, BF16)
    consts = SBA("consts", 16, F32)
    onesb = SBA("onesb", 4, BF16)
    ones1f = SBA("ones1f", 128, F32, parts=1)
    onesf = SBA("onesf", 4, F32)
    ones1b = SBA("ones1b", 64, BF16, parts=1)
    if not _SIM:
        rowb = SBA("rowb", T, BF16, parts=1)
    rowM = SBA("rowM", T, F32, parts=1)
    b1q = SBA("b1q", 32, F32)
    b2q = SBA("b2q", 8, F32)
    b1q8 = SBA("b1q8", 32, FP8)
    b2q8 = SBA("b2q8", 8, FP8)

    if _SIM:
        QW = 32
        qs = {}
        for n_ in ("sq", "l2", "av", "ev", "tmp", "bp", "bm", "r0", "dv", "c1", "c2",
                   "e1", "e2", "e1b", "mu", "me"):
            qs[n_] = SBA("qs_" + n_, QW, F32)
        import jax as _jax
        _jax.config.update("jax_enable_x64", True)
        _cpu = _jax.devices("cpu")[0]
        qs["tie_up"] = []
        qs["tie_even"] = []
        with _jax.default_device(_cpu):
            import jax.numpy as _jnp
            for s_ in range(-45, 20):
                es2v = float(_jnp.exp2(_jnp.float64(s_)))
                beh = []
                for k_ in (4, 5):
                    t_ = np.float64((k_ + 0.5) * (2.0 ** s_)) / np.float64(es2v)
                    beh.append(int(np.round(t_)) - k_)
                if beh == [0, 0]:
                    pass
                elif beh == [1, 1]:
                    qs["tie_up"].append(s_)
                else:
                    qs["tie_even"].append(s_)
        for n_ in ("ei", "ti", "bpi", "bmi"):
            qs[n_] = SBA("qs_" + n_, QW, I32)
    else:
        QW = 512
        qs = None

    for k in p.semkeys:
        nc.vector.sem_clear(p.sems[k])

    C_MAGIC = 6755399441055744.0
    V52 = 4503599627370497.0
    V29 = 536870913.0
    LN2_64 = float(np.log(np.float64(2.0)))

    def emit_quant_sim_blk(x_ap, out_ap, nconst, lo_clamp, reads, writes):
        W = x_ap.shape[1]
        assert W <= QW
        ks = "qscratch"
        sq, l2, av, ev, tmp = qs["sq"], qs["l2"], qs["av"], qs["ev"], qs["tmp"]
        bp, bm, r0, dv, c1, c2 = qs["bp"], qs["bm"], qs["r0"], qs["dv"], qs["c1"], qs["c2"]
        e1, e2, e1b, mu, me = qs["e1"], qs["e2"], qs["e1b"], qs["mu"], qs["me"]

        def V(fn):
            p.op("dve", fn, reads=[ks], writes=[ks])

        if nconst == 7:
            p.op("dve", lambda eng, x_ap=x_ap: eng.tensor_scalar(
                out=e1b[:, :W], in0=x_ap, scalar1=V29, scalar2=None, op0=ALU.mult),
                 reads=list(reads) + [ks], writes=[ks])
            p.op("dve", lambda eng, x_ap=x_ap: eng.scalar_tensor_tensor(
                out=bm[:, :W], in0=x_ap, scalar=-1.0, in1=e1b[:, :W],
                op0=ALU.mult, op1=ALU.add),
                 reads=list(reads) + [ks], writes=[ks])
            V(lambda eng: eng.tensor_tensor(out=e1b[:, :W], in0=e1b[:, :W], in1=bm[:, :W],
                                            op=ALU.subtract))
            x_ap = e1b[:, :W]
            reads = [ks]

        p.op("act", lambda eng, x_ap=x_ap: eng.activation(out=sq[:, :W], in_=x_ap,
                                                          func=AF.Square),
             reads=list(reads), writes=[ks])
        p.op("act", lambda eng: eng.activation(out=l2[:, :W], in_=sq[:, :W], func=AF.Ln,
                                               bias=consts[:, 0:1], scale=1.0),
             reads=[ks, "consts"], writes=[ks])
        V(lambda eng: eng.tensor_scalar(out=av[:, :W], in0=l2[:, :W], scalar1=HALF_INVLN2,
                                        scalar2=-0.5, op0=ALU.mult, op1=ALU.add))
        V(lambda eng: eng.tensor_scalar(out=ev[:, :W], in0=av[:, :W], scalar1=C_MAGIC,
                                        scalar2=None, op0=ALU.add))
        V(lambda eng: eng.tensor_scalar(out=ev[:, :W], in0=ev[:, :W], scalar1=C_MAGIC,
                                        scalar2=None, op0=ALU.subtract))
        V(lambda eng: eng.tensor_scalar(out=tmp[:, :W], in0=ev[:, :W], scalar1=lo_clamp,
                                        scalar2=-1.0, op0=ALU.max, op1=ALU.mult))
        V(lambda eng: eng.tensor_scalar(out=tmp[:, :W], in0=tmp[:, :W], scalar1=float(nconst),
                                        scalar2=None, op0=ALU.add))
        V(lambda eng: eng.tensor_scalar(out=bp[:, :W], in0=tmp[:, :W], scalar1=-LN2_64,
                                        scalar2=None, op0=ALU.mult))
        p.op("act", lambda eng: eng.activation(out=bp[:, :W], in_=bp[:, :W], func=AF.Exp),
             reads=[ks], writes=[ks])
        V(lambda eng: eng.tensor_scalar(out=e1[:, :W], in0=bp[:, :W], scalar1=V52,
                                        scalar2=None, op0=ALU.mult))
        V(lambda eng: eng.scalar_tensor_tensor(out=bm[:, :W], in0=bp[:, :W], scalar=-1.0,
                                               in1=e1[:, :W], op0=ALU.mult, op1=ALU.add))
        V(lambda eng: eng.tensor_tensor(out=e1[:, :W], in0=e1[:, :W], in1=bm[:, :W],
                                        op=ALU.subtract))
        V(lambda eng: eng.tensor_scalar(out=bp[:, :W], in0=tmp[:, :W], scalar1=LN2_64,
                                        scalar2=None, op0=ALU.mult))
        p.op("act", lambda eng: eng.activation(out=bp[:, :W], in_=bp[:, :W], func=AF.Exp),
             reads=[ks], writes=[ks])
        V(lambda eng: eng.tensor_scalar(out=e2[:, :W], in0=bp[:, :W], scalar1=V52,
                                        scalar2=None, op0=ALU.mult))
        V(lambda eng: eng.scalar_tensor_tensor(out=bm[:, :W], in0=bp[:, :W], scalar=-1.0,
                                               in1=e2[:, :W], op0=ALU.mult, op1=ALU.add))
        V(lambda eng: eng.tensor_tensor(out=e2[:, :W], in0=e2[:, :W], in1=bm[:, :W],
                                        op=ALU.subtract))
        p.op("dve", lambda eng, x_ap=x_ap: eng.tensor_tensor(out=tmp[:, :W], in0=x_ap,
                                                              in1=e2[:, :W], op=ALU.mult),
             reads=list(reads) + [ks], writes=[ks])
        V(lambda eng: eng.tensor_scalar(out=r0[:, :W], in0=tmp[:, :W], scalar1=C_MAGIC,
                                        scalar2=None, op0=ALU.add))
        V(lambda eng: eng.tensor_scalar(out=r0[:, :W], in0=r0[:, :W], scalar1=C_MAGIC,
                                        scalar2=None, op0=ALU.subtract))
        if nconst == 2:
            V(lambda eng: eng.tensor_scalar(out=av[:, :W], in0=ev[:, :W], scalar1=lo_clamp,
                                            scalar2=-1.0, op0=ALU.max, op1=ALU.mult))
            V(lambda eng: eng.tensor_scalar(out=av[:, :W], in0=av[:, :W],
                                            scalar1=float(nconst), scalar2=None, op0=ALU.add))
            V(lambda eng: eng.scalar_tensor_tensor(out=dv[:, :W], in0=r0[:, :W], scalar=-1.0,
                                                   in1=tmp[:, :W], op0=ALU.mult, op1=ALU.add))
            V(lambda eng: eng.tensor_scalar(out=c1[:, :W], in0=dv[:, :W], scalar1=0.5,
                                            scalar2=None, op0=ALU.is_equal))
            V(lambda eng: eng.tensor_scalar(out=c2[:, :W], in0=dv[:, :W], scalar1=-0.5,
                                            scalar2=None, op0=ALU.is_equal))
            V(lambda eng: eng.tensor_tensor(out=c2[:, :W], in0=c1[:, :W], in1=c2[:, :W],
                                            op=ALU.add))
            V(lambda eng: eng.tensor_scalar(out=c1[:, :W], in0=tmp[:, :W], scalar1=0.0,
                                            scalar2=None, op0=ALU.is_gt))
            V(lambda eng: eng.tensor_scalar(out=c1[:, :W], in0=c1[:, :W], scalar1=2.0,
                                            scalar2=-1.0, op0=ALU.mult, op1=ALU.add))
            first = True
            for s_ in qs["tie_up"]:
                if first:
                    V(lambda eng, s_=s_: eng.tensor_scalar(
                        out=mu[:, :W], in0=av[:, :W], scalar1=float(-s_), scalar2=None,
                        op0=ALU.is_equal))
                    first = False
                else:
                    V(lambda eng, s_=s_: eng.scalar_tensor_tensor(
                        out=mu[:, :W], in0=av[:, :W], scalar=float(-s_), in1=mu[:, :W],
                        op0=ALU.is_equal, op1=ALU.add))
            if first:
                V(lambda eng: eng.memset(mu[:, :W], 0.0))
            first = True
            for s_ in qs["tie_even"]:
                if first:
                    V(lambda eng, s_=s_: eng.tensor_scalar(
                        out=me[:, :W], in0=av[:, :W], scalar1=float(-s_), scalar2=None,
                        op0=ALU.is_equal))
                    first = False
                else:
                    V(lambda eng, s_=s_: eng.scalar_tensor_tensor(
                        out=me[:, :W], in0=av[:, :W], scalar=float(-s_), in1=me[:, :W],
                        op0=ALU.is_equal, op1=ALU.add))
            if first:
                V(lambda eng: eng.memset(me[:, :W], 0.0))
            V(lambda eng: eng.tensor_scalar(out=av[:, :W], in0=mu[:, :W], scalar1=-1.0,
                                            scalar2=1.0, op0=ALU.mult, op1=ALU.add))
            V(lambda eng: eng.tensor_tensor(out=av[:, :W], in0=av[:, :W], in1=me[:, :W],
                                            op=ALU.subtract))
            V(lambda eng: eng.scalar_tensor_tensor(out=me[:, :W], in0=c1[:, :W], scalar=0.5,
                                                   in1=dv[:, :W], op0=ALU.mult, op1=ALU.add))
            V(lambda eng: eng.scalar_tensor_tensor(out=e1b[:, :W], in0=c1[:, :W], scalar=-0.5,
                                                   in1=dv[:, :W], op0=ALU.mult, op1=ALU.add))
            V(lambda eng: eng.tensor_tensor(out=mu[:, :W], in0=mu[:, :W], in1=me[:, :W],
                                            op=ALU.mult))
            V(lambda eng: eng.tensor_tensor(out=av[:, :W], in0=av[:, :W], in1=e1b[:, :W],
                                            op=ALU.mult))
            V(lambda eng: eng.tensor_tensor(out=mu[:, :W], in0=mu[:, :W], in1=av[:, :W],
                                            op=ALU.add))
            V(lambda eng: eng.tensor_tensor(out=mu[:, :W], in0=mu[:, :W], in1=c2[:, :W],
                                            op=ALU.mult))
            V(lambda eng: eng.tensor_tensor(out=r0[:, :W], in0=r0[:, :W], in1=mu[:, :W],
                                            op=ALU.add))
        p.op("dve", lambda eng: eng.tensor_tensor(out=out_ap, in0=r0[:, :W], in1=e1[:, :W],
                                                  op=ALU.mult),
             reads=[ks], writes=list(writes))

    def emit_quant_sim(x_ap, out_ap, nconst, lo_clamp, reads, writes):
        W = x_ap.shape[1]
        for o in range(0, W, QW):
            w = min(QW, W - o)
            emit_quant_sim_blk(x_ap[:, o:o + w], out_ap[:, o:o + w], nconst, lo_clamp,
                               reads, writes)

    def quantize(x_ap, out_ap, kind, reads, writes):
        if not _SIM:
            p.op("dve", lambda eng: eng.tensor_copy(out=out_ap, in_=x_ap),
                 reads=list(reads), writes=list(writes))
        elif kind == "fq":
            emit_quant_sim(x_ap, out_ap, 2, -14.0, reads, writes)
        else:
            emit_quant_sim(x_ap, out_ap, 7, -50.0, reads, writes)

    p.op("dve", lambda eng: eng.memset(consts[:, 0:1], 1e-38), writes=["consts"])
    p.op("dve", lambda eng: eng.memset(consts[:, 1:2], 1e-5), writes=["consts"])
    p.op("dve", lambda eng: eng.memset(onesb[:, 0:1], 1.0), writes=["onesb"])
    p.op("dve", lambda eng: eng.memset(ones1f[:, :], 1.0), writes=["ones1f"])
    p.op("dve", lambda eng: eng.memset(onesf[:, 0:1], 1.0), writes=["onesf"])
    p.op("dve", lambda eng: eng.memset(ones1b[:, :], 1.0), writes=["ones1b"])
    p.op("sp", lambda eng: eng.dma_start(out=trix[:], in_=d_trix), writes=["trix"], queue="sp")
    p.op("sp", lambda eng: eng.dma_start(out=b1q[:, :], in_=d_b1c), writes=["b1f"], queue="sp")
    p.op("sp", lambda eng: eng.dma_start(out=b2q[:, :], in_=d_b2c), writes=["b2f"], queue="sp")
    quantize(b1q[:, :], b1q8[:, :], "fq", ["b1f"], ["b1q8"])
    quantize(b2q[:, :], b2q8[:, :], "fq", ["b2f"], ["b2q8"])
    p.op("dve", lambda eng: eng.tensor_copy(out=b1q[:, :], in_=b1q8[:, :]),
         reads=["b1q8"], writes=["b1q"])
    p.op("dve", lambda eng: eng.tensor_copy(out=b2q[:, :], in_=b2q8[:, :]),
         reads=["b2q8"], writes=["b2q"])

    keff = SBA("keff", 8 * L, BF16)
    vsb = SBR("vsb", 4 * C, BF16)
    wuqT_b = SBR("wuqT_b", 4096, BF16)
    wdkvT_b = SBR("wdkvT_b", 4096, BF16)
    wdqT_b = SBR("wdqT_b", 4096, BF16)
    stg = SBR("stg", 2 * 4096, F32)
    mt = SBR("mt", 8 * C, BF16)
    wdq_b = SBR("wdq_b", 4096, BF16)
    wuk_b = SBR("wuk_b", 4096, BF16)
    wuv_b = SBR("wuv_b", 4096, BF16)
    woT_b = SBR("woT_b", 8192, BF16)

    def load_cast(dst, dst_key, src_ap, cols, half):
        a = src_ap.shape[0] // 128
        w = cols // a
        sslice = stg[:, half * 4096: half * 4096 + cols]
        skey = f"stg{half}"
        p.op("sp", lambda eng: eng.dma_start(
            out=sslice.rearrange("p (a n) -> p a n", a=a),
            in_=src_ap.rearrange("(a p) n -> p a n", p=128)),
            writes=[skey], queue="sp")
        p.op("dve", lambda eng: eng.tensor_copy(out=dst[:, 0:cols], in_=sslice),
             reads=[skey], writes=[dst_key])

    load_cast(wdqT_b, "wdqT_b", d_wdqT, 4096, 0)
    load_cast(wuqT_b, "wuqT_b", d_wuqT, 4096, 1)
    load_cast(wdq_b, "wdq_b", d_wdq, 4096, 0)
    load_cast(wuk_b, "wuk_b", d_wuk, 4096, 1)
    load_cast(wuv_b, "wuv_b", d_wuv, 4096, 0)
    load_cast(wdkvT_b, "wdkvT_b", d_wdkvT, 4096, 1)
    for h_ in range(2):
        sslice = stg[:, h_ * 4096:(h_ + 1) * 4096]
        skey = f"stg{h_}"
        p.op("sp", lambda eng, s=sslice, h_=h_: eng.dma_start(
            out=s.rearrange("p (a n) -> p a n", n=1024),
            in_=d_woT.rearrange("(b a p) n -> b p a n", b=2, p=128)[h_]),
            writes=[skey], queue="sp")
        p.op("dve", lambda eng, s=sslice, h_=h_: eng.tensor_copy(
            out=woT_b[:, h_ * 4096:(h_ + 1) * 4096], in_=s),
            reads=[skey], writes=["woT_b"])

    def mm_block(dst_ap, dst_key, pairs, read_keys):
        pk, pt = ppool.take()
        nk = len(pairs)
        for i, (la, ra) in enumerate(pairs):
            p.op("pe", lambda eng, la=la, ra=ra, i=i, pt=pt, nk=nk: eng.matmul(
                pt[:, 0:ra.shape[1]], la, ra, start=(i == 0), stop=(i == nk - 1)),
                reads=list(read_keys), writes=[pk])
        p.op("act", lambda eng, pt=pt, o=dst_ap: eng.activation(
            out=o, in_=pt[:, 0:o.shape[1]], func=AF.Copy),
            reads=[pk], writes=[dst_key])

    for jc in range(8):
        for ib in range(2):
            pairs = [(wuqT_b[:, qc * 1024 + jc * 128: qc * 1024 + (jc + 1) * 128],
                      wdq_b[:, qc * 1024 + ib * 512: qc * 1024 + (ib + 1) * 512])
                     for qc in range(4)]
            mm_block(mt[:, jc * 1024 + ib * 512: jc * 1024 + (ib + 1) * 512], "mt",
                     pairs, ["wuqT_b", "wdq_b"])
    for ic in range(8):
        pairs = [(mt[:, jc * 1024 + ic * 128: jc * 1024 + (ic + 1) * 128],
                  wuk_b[:, jc * 512:(jc + 1) * 512]) for jc in range(8)]
        mm_block(keff[:, ic * 512:(ic + 1) * 512], "keff", pairs, ["mt", "wuk_b"])
    for lc in range(4):
        for cb in range(2):
            pairs = [(wuv_b[:, cc * 512 + lc * 128: cc * 512 + (lc + 1) * 128],
                      woT_b[:, cc * 1024 + cb * 512: cc * 1024 + (cb + 1) * 512])
                     for cc in range(8)]
            mm_block(vsb[:, lc * 1024 + cb * 512: lc * 1024 + (cb + 1) * 512], "vsb",
                     pairs, ["wuv_b", "woT_b"])
    SBFREE("woT_b", "wuv_b", "wuk_b", "wdq_b", "mt", "stg")

    xnb = SBR("xnb", 8 * T, BF16)
    xf = SBR("xf", (T if _SIM else 2 * T), F32)
    xnf = SBR("xnf", T, F32)
    rbsb = SBR("rbsb", T, F32)
    mrsb = SBR("mrsb", T, F32)
    rowV = SBR("rowV", T, F32, parts=1)
    rowT = SBR("rowT", T, F32, parts=1)
    for nk in ("xnb", "xf0", "xf1", "xbq", "x2q", "xnf", "rbsb", "mrsb", "lnvar", "lntmp"):
        p.alias(nk, ["mt", "wdq_b", "wuk_b", "wuv_b", "woT_b", "stg0", "stg1"])

    xT_r = d_xT.rearrange("(a p) t -> a p t", p=128)
    for cj in range(8):
        _sl = 0 if _SIM else (cj % 2)
        xf_sl = xf[:, _sl * T:(_sl + 1) * T]
        xkey = f"xf{_sl}"
        p.op("pool", lambda eng, cj=cj, s=xf_sl: eng.dma_start(out=s, in_=xT_r[cj]),
             writes=[xkey], queue="pool")
        p.op("act", lambda eng, s=xf_sl: eng.activation(out=xnf[:, :], in_=s, func=AF.Square),
             reads=[xkey], writes=["xsq"])
        for b in range(NB):
            p.op("pe", lambda eng, b=b, cj=cj, s=xf_sl: eng.matmul(
                psA[0:1, b * 512:(b + 1) * 512], onesf[:, 0:1],
                s[:, b * 512:(b + 1) * 512], start=(cj == 0), stop=(cj == 7)),
                reads=[xkey, "onesf"], writes=[f"psA{b}"])
            p.op("pe", lambda eng, b=b, cj=cj: eng.matmul(
                pool_tiles[b][1][0:1, :], onesf[:, 0:1],
                xnf[:, b * 512:(b + 1) * 512], start=(cj == 0), stop=(cj == 7)),
                reads=["xsq", "onesf"], writes=[pool_tiles[b][0]])
    p.op("dve", lambda eng: eng.tensor_scalar(out=rowM[:, :], in0=psA[0:1, :], scalar1=1.0 / C,
                                              scalar2=None, op0=ALU.mult),
         reads=psA_keys, writes=["lnmean"])
    for b in range(NB):
        p.op("dve", lambda eng, b=b: eng.tensor_scalar(
            out=rowV[:, b * 512:(b + 1) * 512], in0=pool_tiles[b][1][0:1, :],
            scalar1=1.0 / C, scalar2=None, op0=ALU.mult),
            reads=[pool_tiles[b][0]], writes=["lnvar"])
    p.op("act", lambda eng: eng.activation(out=rowT[:, :], in_=rowM[:, :], func=AF.Square),
         reads=["lnmean"], writes=["lntmp"])
    p.op("dve", lambda eng: eng.tensor_tensor(out=rowV[:, :], in0=rowV[:, :], in1=rowT[:, :],
                                              op=ALU.subtract),
         reads=["lnvar", "lntmp"], writes=["lnvar"])
    p.op("act", lambda eng: eng.activation(out=rowV[:, :], in_=rowV[:, :], func=AF.Ln,
                                           bias=consts[0:1, 1:2], scale=1.0),
         reads=["lnvar", "consts"], writes=["lnvar"])
    p.op("act", lambda eng: eng.activation(out=rowV[:, :], in_=rowV[:, :], func=AF.Exp,
                                           scale=-0.5),
         reads=["lnvar"], writes=["lnvar"])
    p.op("dve", lambda eng: eng.tensor_tensor(out=rowT[:, :], in0=rowM[:, :], in1=rowV[:, :],
                                              op=ALU.mult),
         reads=["lnmean", "lnvar"], writes=["lntmp"])
    for b in range(NB):
        pk, pt = ppool.take()
        p.op("pe", lambda eng, b=b, pt=pt: eng.matmul(
            pt[:, :], ones1f[:, :], rowV[:, b * 512:(b + 1) * 512], start=True, stop=True),
            reads=["lnvar", "ones1f"], writes=[pk])
        p.op("dve", lambda eng, b=b, pt=pt: eng.tensor_copy(
            out=rbsb[:, b * 512:(b + 1) * 512], in_=pt[:, :]),
            reads=[pk], writes=["rbsb"])
    for b in range(NB):
        pk2, pt2 = ppool.take()
        p.op("pe", lambda eng, b=b, pt=pt2: eng.matmul(
            pt[:, :], ones1f[:, :], rowT[:, b * 512:(b + 1) * 512], start=True, stop=True),
            reads=["lntmp", "ones1f"], writes=[pk2])
        p.op("dve", lambda eng, b=b, pt=pt2: eng.tensor_copy(
            out=mrsb[:, b * 512:(b + 1) * 512], in_=pt[:, :]),
            reads=[pk2], writes=["mrsb"])
    SBFREE("rowT", "rowV")

    for cj in range(8):
        _sl = 0 if _SIM else (cj % 2)
        xf_sl = xf[:, _sl * T:(_sl + 1) * T]
        xkey = f"xf{_sl}"
        p.op("pool", lambda eng, cj=cj, s=xf_sl: eng.dma_start(out=s, in_=xT_r[cj]),
             writes=[xkey], queue="pool")
        p.op("dve", lambda eng, s=xf_sl: eng.tensor_tensor(out=xnf[:, :], in0=s, in1=rbsb[:, :],
                                                           op=ALU.mult),
             reads=[xkey, "rbsb"], writes=["xnf"])
        p.op("dve", lambda eng: eng.tensor_tensor(out=xnf[:, :], in0=xnf[:, :], in1=mrsb[:, :],
                                                  op=ALU.subtract),
             reads=["xnf", "mrsb"], writes=["xnf"])
        p.op("act", lambda eng, cj=cj: eng.activation(out=xnb[:, cj * T:(cj + 1) * T],
                                                      in_=xnf[:, :], func=AF.Copy),
             reads=["xnf"], writes=["xnb"])
        quantize(xnf[:, :], xn_q[:, cj * T:(cj + 1) * T], "fq", ["xnf"], ["xn_q"])
    dump("g_xnb", xnb[:], ["xnb"])
    dump("g_xnq", xn_q[:], ["xn_q"])
    dump("g_rbsb", rbsb[:], ["rbsb"])
    dump("g_mrsb", mrsb[:], ["mrsb"])
    dump("g_rowM", rowM[:], ["lnmean"])
    SBFREE("mrsb", "rbsb", "xnf", "xf")

    q1b = SBR("q1b", 4 * T, BF16)
    p.alias("q1b", ["xf0", "xf1", "xbq", "x2q", "xnf", "rbsb", "mrsb"])
    for qs_ in range(4):
        for b in range(NB):
            pairs = [(wdqT_b[:, cc * 512 + qs_ * 128: cc * 512 + (qs_ + 1) * 128],
                      xnb[:, cc * T + b * 512: cc * T + (b + 1) * 512]) for cc in range(8)]
            mm_block(q1b[:, qs_ * T + b * 512: qs_ * T + (b + 1) * 512], "q1b",
                     pairs, ["wdqT_b", "xnb"])
    qT = SBA("qT", 8 * T, BF16)
    for cs in range(8):
        for b in range(NB):
            pairs = [(wuqT_b[:, qc * 1024 + cs * 128: qc * 1024 + (cs + 1) * 128],
                      q1b[:, qc * T + b * 512: qc * T + (b + 1) * 512]) for qc in range(4)]
            mm_block(qT[:, cs * T + b * 512: cs * T + (b + 1) * 512], "qT",
                     pairs, ["wuqT_b", "q1b"])
    dump("g_qT", qT[:], ["qT"])
    SBFREE("q1b")
    ckvT = SBA("ckvT", 4 * T, BF16)
    for ls in range(4):
        for b in range(NB):
            pairs = [(wdkvT_b[:, cc * 512 + ls * 128: cc * 512 + (ls + 1) * 128],
                      xnb[:, cc * T + b * 512: cc * T + (b + 1) * 512]) for cc in range(8)]
            mm_block(ckvT[:, ls * T + b * 512: ls * T + (b + 1) * 512], "ckvT",
                     pairs, ["wdkvT_b", "xnb"])
    dump("g_ckvT", ckvT[:], ["ckvT"])
    dump("g_keff", keff[:], ["keff"])
    dump("g_vsb", vsb[:], ["vsb"])
    SBFREE("xnb", "wdqT_b", "wdkvT_b", "wuqT_b")
    usb = SBA("usb", 16 * (NH * 65), BF16)
    for _kc in range(16):
        for _hh in range(NH):
            p.op("dve", lambda eng, _kc=_kc, _hh=_hh: eng.memset(
                usb[:, _kc * 1040 + _hh * 65 + 64: _kc * 1040 + _hh * 65 + 65], 1.0),
                writes=["usb_ones"])
    for kc in range(16):
        for cb in range(2):
            pk, pt = ppool.take()
            for lc in range(4):
                p.op("pe", lambda eng, kc=kc, cb=cb, lc=lc, pt=pt: eng.matmul(
                    pt[:, :], ckvT[:, lc * T + kc * 128: lc * T + (kc + 1) * 128],
                    vsb[:, lc * 1024 + cb * 512: lc * 1024 + (cb + 1) * 512],
                    start=(lc == 0), stop=(lc == 3)),
                    reads=["ckvT", "vsb"], writes=[pk])
            out_ap = usb[:, kc * 1040 + cb * 8 * 65: kc * 1040 + (cb * 8 + 8) * 65] \
                .rearrange("p (h o) -> p h o", h=8)[:, :, 0:64]
            p.op("act", lambda eng, pt=pt, o=out_ap: eng.activation(
                out=o, in_=pt[:, :].rearrange("p (h d) -> p h d", h=8), func=AF.Copy),
                reads=[pk], writes=["usb"])
    dump("g_usb", usb[:], ["usb", "usb_ones"])
    SBFREE("vsb")

    stg2 = SBR("stg2", 4096, F32)
    stg8 = SBR("stg8", 4096, FP8)
    for nk in ("stg2_0", "stg8_0"):
        p.alias(nk, ["q1b", "xnb", "wdqT_b", "wdkvT_b", "wuqT_b"])
    w1T_r = d_w1T.rearrange("(a p) n -> a p n", p=128)
    w1q8_r = d_w1q8.rearrange("(a p) n -> a p n", p=128)
    w2T_r = d_w2T.rearrange("(a p) n -> a p n", p=128)
    w2q8_r = d_w2q8.rearrange("(a p) n -> a p n", p=128)
    wf_steps = []

    def _wf_step(mat_r, q8_r, a, cols):
        def go():
            sslice = stg2[:, 0:cols]
            skey = "stg2_0"
            s8 = stg8[:, 0:cols]
            s8key = "stg8_0"
            dkey = (f"w1q8dram_{a}" if q8_r is w1q8_r else f"w2q8dram_{a}")
            p.op("sp", lambda eng, a=a, s=sslice: eng.dma_start(out=s, in_=mat_r[a]),
                 writes=[skey], queue="sp")
            quantize(sslice, s8, "fq", [skey], [s8key])
            p.op("pool", lambda eng, a=a, s8=s8: eng.dma_start(out=q8_r[a], in_=s8),
                 reads=[s8key], writes=[dkey], queue="pool")
        return go

    for a in range(8):
        wf_steps.append(_wf_step(w1T_r, w1q8_r, a, 4096))
    for a in range(32):
        wf_steps.append(_wf_step(w2T_r, w2q8_r, a, 1024))

    qlat = SBA("qlat", 4 * T, BF16)
    ptb = SBA("ptb", 2 * T, BF16)
    yae = SBA("yae", 1024, F32)
    ynb = SBA("ynb", T, BF16)
    ss32 = SBA("ss32", 32, F32, parts=32)
    rrow = SBA("rrow", 1024, F32, parts=1)
    for nk in ("qlat", "pt0_0", "pt0_1", "pt0_2", "pt0_3", "pt1_0", "pt1_1", "pt1_2",
               "pt1_3", "yae0", "yae1", "ss32", "rrow"):
        p.alias(nk, ["stg2_0", "stg2_1", "stg8_0", "stg8_1", "vsb"])

    def emit_qlat(h):
        b0 = (h % 2) * 64
        hc = h // 2
        for b in range(NB):
            for lc in range(4):
                pk, pt = ppool.take()
                p.op("pe", lambda eng, lc=lc, b=b, pt=pt, b0=b0, hc=hc: eng.matmul(
                    pt[:, :],
                    keff[b0:b0 + 64, hc * 512 + lc * 128: hc * 512 + (lc + 1) * 128],
                    qT[b0:b0 + 64, hc * T + b * 512: hc * T + (b + 1) * 512],
                    start=True, stop=True),
                    reads=["keff", "qT"], writes=[pk])
                p.op("act", lambda eng, lc=lc, b=b, pt=pt: eng.activation(
                    out=qlat[:, lc * T + b * 512: lc * T + (b + 1) * 512], in_=pt[:, :],
                    func=AF.Copy),
                    reads=[pk], writes=[f"qlat_{b}"])

    def emit_qk(h, kt):
        ptile = ptb[:, (kt % 2) * T:(kt % 2 + 1) * T]
        ptkey = f"pt{kt % 2}"
        b_lo = kt // 4
        for b in range(b_lo, NB):
            pk, pt = ppool.take()
            for lc in range(4):
                p.op("pe", lambda eng, kt=kt, b=b, lc=lc, pt=pt: eng.matmul(
                    pt[:, :], ckvT[:, lc * T + kt * 128: lc * T + (kt + 1) * 128],
                    qlat[:, lc * T + b * 512: lc * T + (b + 1) * 512],
                    start=(lc == 0), stop=(lc == 3)),
                    reads=["ckvT", f"qlat_{b}"], writes=[pk])
            p.op("act", lambda eng, b=b, pt=pt, ptile=ptile: eng.activation(
                out=ptile[:, b * 512:(b + 1) * 512], in_=pt[:, :], func=AF.Exp, scale=0.125),
                reads=[pk], writes=[ptkey + f"_{b}"])
            if b == b_lo:
                o = 128 * (kt % 4)
                p.op("dve", lambda eng, b=b, ptile=ptile, o=o: eng.tensor_tensor(
                    out=ptile[:, b * 512:(b + 1) * 512], in0=ptile[:, b * 512:(b + 1) * 512],
                    in1=trix[:, 384 - o: 896 - o], op=ALU.mult),
                    reads=[ptkey + f"_{b}", "trix"], writes=[ptkey + f"_{b}"])

    def emit_yatt(h, kt):
        ptile = ptb[:, (kt % 2) * T:(kt % 2 + 1) * T]
        ptkey = f"pt{kt % 2}"
        b_lo = kt // 4
        for b in range(b_lo, NB):
            p.op("pe", lambda eng, kt=kt, b=b, h=h, ptile=ptile: eng.matmul(
                psA[0:65, b * 512:(b + 1) * 512],
                usb[:, kt * 1040 + h * 65: kt * 1040 + (h + 1) * 65],
                ptile[:, b * 512:(b + 1) * 512],
                start=(kt == 0), stop=(kt == min(15, 4 * b + 3))),
                reads=[ptkey + f"_{b}", "usb", "usb_ones"], writes=[f"psA{b}"])

    def emit_epi_block(h, b):
        if h == 0 and b == 0:
            for bb in range(NB):
                p.alias(f"sumsrow_{bb}", ["lnmean"])
        p.op("act", lambda eng, b=b: eng.activation(
            out=rowM[:, b * 512:(b + 1) * 512], in_=psA[64:65, b * 512:(b + 1) * 512],
            func=AF.Copy),
            reads=[f"psA{b}"], writes=[f"sumsrow_{b}"])
        ya_sl = yae[:, (b % 2) * 512:(b % 2 + 1) * 512]
        p.op("act", lambda eng, b=b, ya=ya_sl: eng.activation(
            out=ya[0:64, :], in_=psA[0:64, b * 512:(b + 1) * 512], func=AF.Copy),
            reads=[f"psA{b}"], writes=[f"yae{b % 2}"])

    def emit_epi_recip(h, pair):
        blo = 2 * pair
        p.op("act", lambda eng, blo=blo: eng.dma_start(
            out=ss32[:, :], in_=rowM[0:1, blo * 512:(blo + 2) * 512]),
            reads=[f"sumsrow_{blo}", f"sumsrow_{blo + 1}"], writes=["ss32"], queue="act")
        p.op("dve", lambda eng: eng.reciprocal(out=ss32[:, :], in_=ss32[:, :]),
             reads=["ss32"], writes=["ss32"])
        p.op("act", lambda eng: eng.dma_start(
            out=rrow[0:1, :], in_=ss32[:, :]),
            reads=["ss32"], writes=["rrow"], queue="act")

    def emit_epi_norm(h, pair):
        b0 = (h % 2) * 64
        hc = h // 2
        for b in (2 * pair, 2 * pair + 1):
            pk, pt = ppool.take()
            p.op("pe", lambda eng, b=b, pt=pt: eng.matmul(
                pt[0:64, :], ones1f[:, 0:64], rrow[:, (b % 2) * 512:(b % 2 + 1) * 512],
                start=True, stop=True),
                reads=["rrow", "ones1f"], writes=[pk])
            ya_sl = yae[:, (b % 2) * 512:(b % 2 + 1) * 512]
            if b0 == 0:
                p.op("dve", lambda eng, b=b, pt=pt, ya=ya_sl, hc=hc: eng.tensor_tensor(
                    out=yatt[0:64, hc * T + b * 512: hc * T + (b + 1) * 512],
                    in0=ya[0:64, :], in1=pt[0:64, :], op=ALU.mult),
                    reads=[f"yae{b % 2}", pk], writes=["yatt"])
            else:
                p.op("dve", lambda eng, b=b, pt=pt, ya=ya_sl: eng.tensor_tensor(
                    out=ynb[0:64, b * 512:(b + 1) * 512],
                    in0=ya[0:64, :], in1=pt[0:64, :], op=ALU.mult),
                    reads=[f"yae{b % 2}", pk], writes=["ynb"])
        if b0 == 64 and pair == 1:
            p.op("pool", lambda eng, hc=hc: eng.dma_start(
                out=yatt[64:128, hc * T:(hc + 1) * T], in_=ynb[0:64, :]),
                reads=["ynb"], writes=["yatt"], queue="pool")

    wfi = 0
    tail_prev = None
    for h in range(NH):
        emit_qlat(h)
        if h == 0:
            dump("g_qlat", qlat[:], [f"qlat_{b}" for b in range(NB)])
        prev = None
        for kt in range(16):
            emit_qk(h, kt)
            if h == 0 and kt == 1:
                dump("g_pt", ptb[:], ["pt0_0", "pt0_1", "pt0_2", "pt0_3",
                                      "pt1_0", "pt1_1", "pt1_2", "pt1_3"])
            if kt == 1 and tail_prev is not None:
                tail_prev()
                tail_prev = None
            if prev is not None:
                emit_yatt(h, prev)
                if prev % 4 == 3 and prev < 15:
                    emit_epi_block(h, prev // 4)
            if kt == 9:
                emit_epi_recip(h, 0)
            if kt == 11:
                emit_epi_norm(h, 0)
            prev = kt
        emit_yatt(h, prev)
        emit_epi_block(h, 3)
        emit_epi_recip(h, 1)
        tail_prev = (lambda h=h: emit_epi_norm(h, 1))
        nsteps = (len(wf_steps) * min(h + 1, NH - 2)) // (NH - 2) - wfi
        for _ in range(nsteps):
            wf_steps[wfi]()
            wfi += 1
    tail_prev()
    assert wfi == len(wf_steps)
    dump("g_yatt", yatt[:], ["yatt"])
    SBFREE("stg8", "stg2", barrier=False)

    SBFREE("rrow", "ss32", "ynb", "yae", "ptb", "qlat", "usb", "ckvT", "qT", "keff",
           barrier=False)
    w1q = SBA("w1q", 8 * N, FP8)
    w2q = SBA("w2q", 32 * C, FP8)
    hq = SBA("hq", 32 * 512, FP8)
    hbf = SBA("hbf", 2 * 512, BF16)
    mstg = SBA("mstg", 4096, F32)
    xstg = SBA("xstg", 4096, F32)
    ostg = SBA("ostg", 4096, F32)
    otmp = SBA("otmp", 512, F32)
    y2bf = SBA("y2bf", 512, BF16)
    if _SIM:
        h1f = SBA("h1f", 512, F32)
        y2f = SBA("y2f", 512, F32)
    qlat_keys = [f"qlat_{b}" for b in range(NB)]
    pt_keys = ["pt0_0", "pt0_1", "pt0_2", "pt0_3", "pt1_0", "pt1_1", "pt1_2", "pt1_3"]
    usb_keys = ["usb", "usb_ones"]
    old_keys = ["xnb", "xf0", "xf1", "xnf", "xsq", "rbsb", "mrsb", "lnvar", "lntmp",
                "q1b", "vsb", "stg0", "stg1", "stg2_0", "stg8_0"]
    for a in range(8):
        p.alias(f"w1q_{a}", ["keff", "qT"])
    for a in range(32):
        ks = (["qT"] if a <= 8 else []) + (["ckvT"] if 7 <= a <= 24 else []) \
            + (usb_keys if a >= 23 else [])
        p.alias(f"w2q_{a}", ks)
    p.alias("hq", usb_keys)
    p.alias("hbf0", usb_keys)
    p.alias("hbf1", usb_keys)
    p.alias("mstg", usb_keys + qlat_keys)
    p.alias("xstg", qlat_keys + pt_keys + ["yae0", "yae1"])
    for cs in range(8):
        p.alias(f"ostg{cs}", ["yae0", "yae1", "ynb", "ss32", "rrow"] + old_keys)
    for nk in ("otmp", "y2bf", "h1f", "y2f"):
        p.alias(nk, ["ynb", "ss32", "rrow"] + old_keys)

    for a in range(8):
        p.op("sp", lambda eng, a=a: eng.dma_start(
            out=w1q[:, a * N:(a + 1) * N], in_=w1q8_r[a]),
            reads=[f"w1q8dram_{a}"], writes=[f"w1q_{a}"], queue="sp")
    for a in range(32):
        q = "sp" if a % 2 == 0 else "pool"
        p.op(q, lambda eng, a=a: eng.dma_start(
            out=w2q[:, a * C:(a + 1) * C], in_=w2q8_r[a]),
            reads=[f"w2q8dram_{a}"], writes=[f"w2q_{a}"], queue=q)

    mask_r = d_maskT.rearrange("(a p) (q t) -> q p a t", p=128, t=512)
    x_r2 = d_xT.rearrange("(a p) (q t) -> q p a t", p=128, t=512)
    out_r = d_outT.rearrange("(a p) (q t) -> q p a t", p=128, t=512)

    for ti in range(NB):
        p.op("sp", lambda eng, ti=ti: eng.dma_start(
            out=mstg[:].rearrange("p (a t) -> p a t", t=512), in_=mask_r[ti]),
            writes=["mstg"], queue="sp")
        p.op("pool", lambda eng, ti=ti: eng.dma_start(
            out=xstg[:].rearrange("p (a t) -> p a t", t=512), in_=x_r2[ti]),
            writes=["xstg"], queue="pool")
        for nj in range(32):
            pk, pt = ppool.take()
            for cc in range(8):
                p.op("pe", lambda eng, nj=nj, cc=cc, ti=ti, pt=pt: eng.matmul(
                    pt[:, :], w1q[:, cc * N + nj * 128: cc * N + (nj + 1) * 128],
                    xn_q[:, cc * T + ti * 512: cc * T + (ti + 1) * 512],
                    start=(cc == 0), stop=(cc == 7)),
                    reads=[f"w1q_{cc}", "xn_q"], writes=[pk])
            hbf_sl = hbf[:, (nj % 2) * 512:(nj % 2 + 1) * 512]
            hbkey = f"hbf{nj % 2}"
            if not _SIM:
                p.op("act", lambda eng, nj=nj, pt=pt, o=hbf_sl: eng.activation(
                    out=o, in_=pt[:, :], func=AF.Relu, bias=b1q[:, nj:nj + 1], scale=1.0),
                    reads=[pk, "b1q"], writes=[hbkey])
                p.op("dve", lambda eng, nj=nj, o=hbf_sl: eng.tensor_copy(
                    out=hq[:, nj * 512:(nj + 1) * 512], in_=o),
                    reads=[hbkey], writes=["hq"])
            else:
                p.op("act", lambda eng, nj=nj, pt=pt: eng.activation(
                    out=h1f[:, :], in_=pt[:, :], func=AF.Relu, bias=b1q[:, nj:nj + 1],
                    scale=1.0),
                    reads=[pk, "b1q"], writes=["h1f"])
                emit_quant_sim(h1f[:, :], hbf_sl, 7, -50.0, ["h1f"], [hbkey])
                emit_quant_sim(hbf_sl, hq[:, nj * 512:(nj + 1) * 512], 2, -14.0,
                               [hbkey], ["hq"])
        if ti == 3:
            dump("g_hq", hq[:], ["hq"])
        for cs in range(8):
            pk, pt = ppool.take()
            for nj in range(32):
                p.op("pe", lambda eng, cs=cs, nj=nj, pt=pt: eng.matmul(
                    pt[:, :], w2q[:, nj * C + cs * 128: nj * C + (cs + 1) * 128],
                    hq[:, nj * 512:(nj + 1) * 512],
                    start=(nj == 0), stop=(nj == 31)),
                    reads=[f"w2q_{nj}", "hq"], writes=[pk])
            if not _SIM:
                p.op("dve", lambda eng, cs=cs, pt=pt: eng.tensor_scalar(
                    out=y2bf[:, :], in0=pt[:, :], scalar1=b2q[:, cs:cs + 1], scalar2=None,
                    op0=ALU.add),
                    reads=[pk, "b2q"], writes=["y2bf"])
            else:
                p.op("dve", lambda eng, cs=cs, pt=pt: eng.tensor_scalar(
                    out=y2f[:, :], in0=pt[:, :], scalar1=b2q[:, cs:cs + 1], scalar2=None,
                    op0=ALU.add),
                    reads=[pk, "b2q"], writes=["y2f"])
                emit_quant_sim(y2f[:, :], y2bf[:, :], 7, -50.0, ["y2f"], ["y2bf"])
            p.op("dve", lambda eng, cs=cs: eng.tensor_tensor(
                out=otmp[:, :], in0=y2bf[:, :], in1=mstg[:, cs * 512:(cs + 1) * 512],
                op=ALU.mult),
                reads=["y2bf", "mstg"], writes=["otmp"])
            p.op("dve", lambda eng, cs=cs, ti=ti: eng.tensor_tensor(
                out=otmp[:, :], in0=otmp[:, :],
                in1=yatt[:, cs * T + ti * 512: cs * T + (ti + 1) * 512], op=ALU.add),
                reads=["otmp", "yatt"], writes=["otmp"])
            p.op("dve", lambda eng, cs=cs: eng.tensor_tensor(
                out=ostg[:, cs * 512:(cs + 1) * 512], in0=otmp[:, :],
                in1=xstg[:, cs * 512:(cs + 1) * 512], op=ALU.add),
                reads=["otmp", "xstg"], writes=[f"ostg{cs}"])
        p.op("pool", lambda eng, ti=ti: eng.dma_start(
            out=out_r[ti], in_=ostg[:].rearrange("p (a t) -> p a t", t=512)),
            reads=[f"ostg{cs_}" for cs_ in range(8)], writes=["outdram"], queue="pool")

    p.wait_only("pool", ["outdram"])
    p.wait_only("sp", ["outdram"])

    with nc.Block() as block:
        p.finish(block)
    es.close()
    return nc


def _host_layout(inputs):
    x = np.asarray(inputs["L0_attn_x"], dtype=np.float32)
    wdq = np.asarray(inputs["L0_attn_W_dq"], dtype=np.float32)
    wuq = np.asarray(inputs["L0_attn_W_uq"], dtype=np.float32)
    wdkv = np.asarray(inputs["L0_attn_W_dkv"], dtype=np.float32)
    wuk = np.asarray(inputs["L0_attn_W_uk"], dtype=np.float32)
    wuv = np.asarray(inputs["L0_attn_W_uv"], dtype=np.float32)
    wo = np.asarray(inputs["L0_attn_W_o"], dtype=np.float32)
    w1 = np.asarray(inputs["L0_ffn_W1"], dtype=np.float32)
    b1 = np.asarray(inputs["L0_ffn_b1"], dtype=np.float32)
    w2 = np.asarray(inputs["L0_ffn_W2"], dtype=np.float32)
    b2 = np.asarray(inputs["L0_ffn_b2"], dtype=np.float32)

    import jax
    cpu = jax.devices("cpu")[0]
    if _SIM:
        jax.config.update("jax_enable_x64", True)
    with jax.default_device(cpu):
        k42 = jax.random.key(42, impl="threefry2x32")
        keep = np.asarray(jax.random.bernoulli(k42, 0.8, (8, T, C)))
    maskv = np.where(keep, np.float32(1.25), np.float32(0.0)).astype(np.float32)

    kk = np.arange(128)[:, None]
    mm = np.arange(896)[None, :]
    trix = (mm >= kk + 384).astype(ml_dtypes.bfloat16)

    com = {
        "d_wdq": np.ascontiguousarray(wdq),
        "d_wdqT": np.ascontiguousarray(wdq.T),
        "d_wuqT": np.ascontiguousarray(wuq.T),
        "d_wuk": np.ascontiguousarray(wuk),
        "d_wuv": np.ascontiguousarray(wuv),
        "d_wdkvT": np.ascontiguousarray(wdkv.T),
        "d_woT": np.ascontiguousarray(wo.T),
        "d_w1T": np.ascontiguousarray(w1.T),
        "d_w2T": np.ascontiguousarray(w2.T),
        "d_b1c": np.ascontiguousarray(b1.reshape(32, 128).T),
        "d_b2c": np.ascontiguousarray(b2.reshape(8, 128).T),
        "d_trix": trix,
    }
    in_maps = []
    for core in range(8):
        m = dict(com)
        m["d_xT"] = np.ascontiguousarray(x[core].T)
        m["d_maskT"] = np.ascontiguousarray(maskv[core].T)
        in_maps.append(m)
    return in_maps


def kernel(**inputs):
    in_maps = _host_layout(inputs)
    nc = build_program()
    res = run_bass_kernel_spmd(nc, in_maps, core_ids=list(range(8)))
    out = np.stack([np.asarray(res.results[i]["d_outT"]) for i in range(8)], axis=0)
    return np.ascontiguousarray(out.transpose(0, 2, 1))


# version 33 (final)
# speedup vs baseline: 1.1678x; 1.1030x over previous
import os
import numpy as np
import ml_dtypes
import concourse.bass as bass
import concourse.mybir as mybir
from concourse.bass_utils import run_bass_kernel_spmd

F32 = mybir.dt.float32
BF16 = mybir.dt.bfloat16
FP8 = mybir.dt.float8e5
I32 = mybir.dt.int32
AF = mybir.ActivationFunctionType
ALU = mybir.AluOpType

T = 2048
C = 1024
L = 512
QL = 512
NH = 16
HS = 64
N = 4096
NB = T // 512
M23 = 8388608.0
B127 = 1065353216.0
HALF_INVLN2 = float(np.float32(0.5 / np.log(2.0)))

_SIM = os.environ.get("BASS_SIM_FP64") == "1"

ENGS = ("pe", "act", "dve", "pool", "sp")
BLOCK_ATTR = {"pe": "tensor", "act": "scalar", "dve": "vector", "pool": "gpsimd", "sp": "sync"}


class Prog:

    def __init__(self, nc, es):
        self.nc = nc
        self.eng = {"pe": nc.tensor, "act": nc.scalar, "dve": nc.vector,
                    "pool": nc.gpsimd, "sp": nc.sync}
        self.semkeys = ["pe", "act", "dve", "pool", "sp"]
        for q in ("sp", "pool", "act"):
            for i in range(8):
                self.semkeys.append(f"d{q}{i}")
        self.sems = {k: es.enter_context(nc.semaphore("s_" + k)) for k in self.semkeys}
        self.cnt = {k: 0 for k in self.semkeys}
        self.dslot = {"sp": 0, "pool": 0, "act": 0}
        self.steps = {e: [] for e in ENGS}
        self.waited = {e: {} for e in ENGS}
        self.bufs = {}

    def buf(self, key):
        b = self.bufs.get(key)
        if b is None:
            b = {"w": None, "r": dict(getattr(self, "birth", {}))}
            self.bufs[key] = b
        return b

    def mark_free_barrier(self):
        ev = dict(getattr(self, "birth", {}))
        for b in self.bufs.values():
            evs = ([b["w"]] if b["w"] else []) + list(b["r"].items())
            for s, v in evs:
                if ev.get(s, 0) < v:
                    ev[s] = v
        self.birth = ev

    def alias(self, new, olds):
        b = self.buf(new)
        for o in olds:
            ob = self.bufs.get(o)
            if not ob:
                continue
            evs = ([ob["w"]] if ob["w"] else []) + list(ob["r"].items())
            for s, v in evs:
                if b["r"].get(s, 0) < v:
                    b["r"][s] = v

    def op(self, e, fn, reads=(), writes=(), queue=None):
        need = {}

        def merge(ev):
            if ev is None:
                return
            s, v = ev
            if need.get(s, 0) < v:
                need[s] = v

        for k in reads:
            merge(self.buf(k)["w"])
        for k in writes:
            b = self.buf(k)
            merge(b["w"])
            for s, v in b["r"].items():
                merge((s, v))
        if queue is not None:
            slot = self.dslot[queue]
            self.dslot[queue] = (slot + 1) % 8
            semkey = f"d{queue}{slot}"
            inc = 16
            if self.cnt[semkey] > 0:
                if need.get(semkey, 0) < self.cnt[semkey]:
                    need[semkey] = self.cnt[semkey]
        else:
            semkey = e
            inc = 1
        wt = self.waited[e]
        waits = []
        for s in sorted(need):
            v = need[s]
            if s == e and not _SIM:
                continue
            if wt.get(s, 0) < v:
                waits.append((self.sems[s], v))
                wt[s] = v
        semobj = self.sems[semkey]

        def step(eng, waits=waits, fn=fn, semobj=semobj, inc=inc):
            for so, v in waits:
                eng.wait_ge(so, v)
            fn(eng).then_inc(semobj, inc)

        self.steps[e].append(step)
        val = self.cnt[semkey] + inc
        self.cnt[semkey] = val
        for k in reads:
            b = self.buf(k)
            if b["r"].get(semkey, 0) < val:
                b["r"][semkey] = val
        for k in writes:
            b = self.buf(k)
            b["w"] = (semkey, val)
            b["r"] = {}

    def wait_only(self, e, keys):
        need = {}
        for k in keys:
            b = self.buf(k)
            if b["w"] is not None:
                s, v = b["w"]
                if need.get(s, 0) < v:
                    need[s] = v
        wt = self.waited[e]
        waits = []
        for s in sorted(need):
            v = need[s]
            if wt.get(s, 0) < v:
                waits.append((self.sems[s], v))
                wt[s] = v

        def step(eng, waits=waits):
            for so, v in waits:
                eng.wait_ge(so, v)

        self.steps[e].append(step)

    def finish(self, block):
        for e in ENGS:
            lst = self.steps[e]

            def replay(eng, lst=lst):
                for s_ in lst:
                    s_(eng)

            getattr(block, BLOCK_ATTR[e])(replay)


class PsumPool:
    def __init__(self, tiles):
        self.tiles = tiles
        self.i = 0

    def take(self):
        k, t_ = self.tiles[self.i % len(self.tiles)]
        self.i += 1
        return k, t_


DBG = False

def build_program():
    nc = bass.Bass("TRN2", target_bir_lowering=False)

    d_xT = nc.dram_tensor("d_xT", [C, T], F32, kind="ExternalInput").ap()
    d_maskT = nc.dram_tensor("d_maskT", [C, T], F32, kind="ExternalInput").ap()
    d_wdq = nc.dram_tensor("d_wdq", [QL, C], F32, kind="ExternalInput").ap()
    d_wdqT = nc.dram_tensor("d_wdqT", [C, QL], F32, kind="ExternalInput").ap()
    d_wuqT = nc.dram_tensor("d_wuqT", [QL, C], F32, kind="ExternalInput").ap()
    d_wuk = nc.dram_tensor("d_wuk", [C, L], F32, kind="ExternalInput").ap()
    d_wuv = nc.dram_tensor("d_wuv", [C, L], F32, kind="ExternalInput").ap()
    d_wdkvT = nc.dram_tensor("d_wdkvT", [C, L], F32, kind="ExternalInput").ap()
    d_woT = nc.dram_tensor("d_woT", [C, C], F32, kind="ExternalInput").ap()
    d_w1T = nc.dram_tensor("d_w1T", [C, N], F32, kind="ExternalInput").ap()
    d_w2T = nc.dram_tensor("d_w2T", [N, C], F32, kind="ExternalInput").ap()
    d_b1c = nc.dram_tensor("d_b1c", [128, 32], F32, kind="ExternalInput").ap()
    d_b2c = nc.dram_tensor("d_b2c", [128, 8], F32, kind="ExternalInput").ap()
    d_trix = nc.dram_tensor("d_trix", [128, 896], BF16, kind="ExternalInput").ap()
    d_outT = nc.dram_tensor("d_outT", [C, T], F32, kind="ExternalOutput").ap()
    d_w1q8 = nc.dram_tensor("d_w1q8", [8 * 128, N], FP8, kind="Internal").ap()
    d_w2q8 = nc.dram_tensor("d_w2q8", [32 * 128, C], FP8, kind="Internal").ap()
    dbg = {}
    if DBG:
        for nm, shp, dt in [("g_xnb", [128, 8 * T], BF16), ("g_xnq", [128, 8 * T], FP8),
                            ("g_qT", [128, 8 * T], BF16), ("g_ckvT", [128, 4 * T], BF16),
                            ("g_keff", [128, 8 * L], BF16), ("g_vsb", [128, 4 * C], BF16),
                            ("g_usb", [128, 16 * NH * 65], BF16),
                            ("g_qlat", [128, 4 * T], BF16), ("g_pt", [128, 2 * T], BF16),
                            ("g_yatt", [128, 8 * T], BF16), ("g_hq", [128, 32 * 512], FP8),
                            ("g_rbsb", [128, T], F32), ("g_mrsb", [128, T], F32),
                            ("g_rowM", [1, T], F32), ("g_rowV", [1, T], F32)]:
            dbg[nm] = nc.dram_tensor(nm, shp, dt, kind="ExternalOutput").ap()

    def dump(nm, sb_ap, keys):
        if DBG:
            p.op("sp", lambda eng: eng.dma_start(out=dbg[nm], in_=sb_ap),
                 reads=list(keys), writes=["dbg_" + nm], queue="sp")

    from contextlib import ExitStack
    es = ExitStack()
    p = Prog(nc, es)

    psA = es.enter_context(nc.psum_tensor("psA", [128, 2048], F32))
    pool_tiles = [(f"psP{i}", es.enter_context(nc.psum_tensor(f"psP{i}", [128, 512], F32)))
                  for i in range(4)]
    ppool = PsumPool(pool_tiles)
    psA_keys = [f"psA{i}" for i in range(4)]

    _live = {}

    def SBA(name, cols, dt, parts=128, side="left"):
        cm = nc.sbuf_tensor(name, [parts, cols], dt, side=side)
        h = cm.__enter__()
        _live[name] = cm
        return h

    def SBR(name, cols, dt, parts=128):
        return SBA(name, cols, dt, parts=parts, side="right")

    def SBFREE(*names, barrier=True):
        for nm in names:
            _live.pop(nm).__exit__(None, None, None)
        if barrier:
            p.mark_free_barrier()

    yatt = SBA("yatt", 8 * T, BF16)
    xn_q = SBA("xn_q", 8 * T, FP8)
    trix = SBA("trix_sb", 896, BF16)
    consts = SBA("consts", 16, F32)
    onesb = SBA("onesb", 4, BF16)
    ones1f = SBA("ones1f", 128, F32, parts=1)
    onesf = SBA("onesf", 4, F32)
    ones1b = SBA("ones1b", 64, BF16, parts=1)
    if not _SIM:
        rowb = SBA("rowb", T, BF16, parts=1)
    rowM = SBA("rowM", T, F32, parts=1)
    b1q = SBA("b1q", 32, F32)
    b2q = SBA("b2q", 8, F32)
    b1q8 = SBA("b1q8", 32, FP8)
    b2q8 = SBA("b2q8", 8, FP8)

    if _SIM:
        QW = 32
        qs = {}
        for n_ in ("sq", "l2", "av", "ev", "tmp", "bp", "bm", "r0", "dv", "c1", "c2",
                   "e1", "e2", "e1b", "mu", "me"):
            qs[n_] = SBA("qs_" + n_, QW, F32)
        import jax as _jax
        _jax.config.update("jax_enable_x64", True)
        _cpu = _jax.devices("cpu")[0]
        qs["tie_up"] = []
        qs["tie_even"] = []
        with _jax.default_device(_cpu):
            import jax.numpy as _jnp
            for s_ in range(-45, 20):
                es2v = float(_jnp.exp2(_jnp.float64(s_)))
                beh = []
                for k_ in (4, 5):
                    t_ = np.float64((k_ + 0.5) * (2.0 ** s_)) / np.float64(es2v)
                    beh.append(int(np.round(t_)) - k_)
                if beh == [0, 0]:
                    pass
                elif beh == [1, 1]:
                    qs["tie_up"].append(s_)
                else:
                    qs["tie_even"].append(s_)
        for n_ in ("ei", "ti", "bpi", "bmi"):
            qs[n_] = SBA("qs_" + n_, QW, I32)
    else:
        QW = 512
        qs = None

    for k in p.semkeys:
        nc.vector.sem_clear(p.sems[k])

    C_MAGIC = 6755399441055744.0
    V52 = 4503599627370497.0
    V29 = 536870913.0
    LN2_64 = float(np.log(np.float64(2.0)))

    def emit_quant_sim_blk(x_ap, out_ap, nconst, lo_clamp, reads, writes):
        W = x_ap.shape[1]
        assert W <= QW
        ks = "qscratch"
        sq, l2, av, ev, tmp = qs["sq"], qs["l2"], qs["av"], qs["ev"], qs["tmp"]
        bp, bm, r0, dv, c1, c2 = qs["bp"], qs["bm"], qs["r0"], qs["dv"], qs["c1"], qs["c2"]
        e1, e2, e1b, mu, me = qs["e1"], qs["e2"], qs["e1b"], qs["mu"], qs["me"]

        def V(fn):
            p.op("dve", fn, reads=[ks], writes=[ks])

        if nconst == 7:
            p.op("dve", lambda eng, x_ap=x_ap: eng.tensor_scalar(
                out=e1b[:, :W], in0=x_ap, scalar1=V29, scalar2=None, op0=ALU.mult),
                 reads=list(reads) + [ks], writes=[ks])
            p.op("dve", lambda eng, x_ap=x_ap: eng.scalar_tensor_tensor(
                out=bm[:, :W], in0=x_ap, scalar=-1.0, in1=e1b[:, :W],
                op0=ALU.mult, op1=ALU.add),
                 reads=list(reads) + [ks], writes=[ks])
            V(lambda eng: eng.tensor_tensor(out=e1b[:, :W], in0=e1b[:, :W], in1=bm[:, :W],
                                            op=ALU.subtract))
            x_ap = e1b[:, :W]
            reads = [ks]

        p.op("act", lambda eng, x_ap=x_ap: eng.activation(out=sq[:, :W], in_=x_ap,
                                                          func=AF.Square),
             reads=list(reads), writes=[ks])
        p.op("act", lambda eng: eng.activation(out=l2[:, :W], in_=sq[:, :W], func=AF.Ln,
                                               bias=consts[:, 0:1], scale=1.0),
             reads=[ks, "consts"], writes=[ks])
        V(lambda eng: eng.tensor_scalar(out=av[:, :W], in0=l2[:, :W], scalar1=HALF_INVLN2,
                                        scalar2=-0.5, op0=ALU.mult, op1=ALU.add))
        V(lambda eng: eng.tensor_scalar(out=ev[:, :W], in0=av[:, :W], scalar1=C_MAGIC,
                                        scalar2=None, op0=ALU.add))
        V(lambda eng: eng.tensor_scalar(out=ev[:, :W], in0=ev[:, :W], scalar1=C_MAGIC,
                                        scalar2=None, op0=ALU.subtract))
        V(lambda eng: eng.tensor_scalar(out=tmp[:, :W], in0=ev[:, :W], scalar1=lo_clamp,
                                        scalar2=-1.0, op0=ALU.max, op1=ALU.mult))
        V(lambda eng: eng.tensor_scalar(out=tmp[:, :W], in0=tmp[:, :W], scalar1=float(nconst),
                                        scalar2=None, op0=ALU.add))
        V(lambda eng: eng.tensor_scalar(out=bp[:, :W], in0=tmp[:, :W], scalar1=-LN2_64,
                                        scalar2=None, op0=ALU.mult))
        p.op("act", lambda eng: eng.activation(out=bp[:, :W], in_=bp[:, :W], func=AF.Exp),
             reads=[ks], writes=[ks])
        V(lambda eng: eng.tensor_scalar(out=e1[:, :W], in0=bp[:, :W], scalar1=V52,
                                        scalar2=None, op0=ALU.mult))
        V(lambda eng: eng.scalar_tensor_tensor(out=bm[:, :W], in0=bp[:, :W], scalar=-1.0,
                                               in1=e1[:, :W], op0=ALU.mult, op1=ALU.add))
        V(lambda eng: eng.tensor_tensor(out=e1[:, :W], in0=e1[:, :W], in1=bm[:, :W],
                                        op=ALU.subtract))
        V(lambda eng: eng.tensor_scalar(out=bp[:, :W], in0=tmp[:, :W], scalar1=LN2_64,
                                        scalar2=None, op0=ALU.mult))
        p.op("act", lambda eng: eng.activation(out=bp[:, :W], in_=bp[:, :W], func=AF.Exp),
             reads=[ks], writes=[ks])
        V(lambda eng: eng.tensor_scalar(out=e2[:, :W], in0=bp[:, :W], scalar1=V52,
                                        scalar2=None, op0=ALU.mult))
        V(lambda eng: eng.scalar_tensor_tensor(out=bm[:, :W], in0=bp[:, :W], scalar=-1.0,
                                               in1=e2[:, :W], op0=ALU.mult, op1=ALU.add))
        V(lambda eng: eng.tensor_tensor(out=e2[:, :W], in0=e2[:, :W], in1=bm[:, :W],
                                        op=ALU.subtract))
        p.op("dve", lambda eng, x_ap=x_ap: eng.tensor_tensor(out=tmp[:, :W], in0=x_ap,
                                                              in1=e2[:, :W], op=ALU.mult),
             reads=list(reads) + [ks], writes=[ks])
        V(lambda eng: eng.tensor_scalar(out=r0[:, :W], in0=tmp[:, :W], scalar1=C_MAGIC,
                                        scalar2=None, op0=ALU.add))
        V(lambda eng: eng.tensor_scalar(out=r0[:, :W], in0=r0[:, :W], scalar1=C_MAGIC,
                                        scalar2=None, op0=ALU.subtract))
        if nconst == 2:
            V(lambda eng: eng.tensor_scalar(out=av[:, :W], in0=ev[:, :W], scalar1=lo_clamp,
                                            scalar2=-1.0, op0=ALU.max, op1=ALU.mult))
            V(lambda eng: eng.tensor_scalar(out=av[:, :W], in0=av[:, :W],
                                            scalar1=float(nconst), scalar2=None, op0=ALU.add))
            V(lambda eng: eng.scalar_tensor_tensor(out=dv[:, :W], in0=r0[:, :W], scalar=-1.0,
                                                   in1=tmp[:, :W], op0=ALU.mult, op1=ALU.add))
            V(lambda eng: eng.tensor_scalar(out=c1[:, :W], in0=dv[:, :W], scalar1=0.5,
                                            scalar2=None, op0=ALU.is_equal))
            V(lambda eng: eng.tensor_scalar(out=c2[:, :W], in0=dv[:, :W], scalar1=-0.5,
                                            scalar2=None, op0=ALU.is_equal))
            V(lambda eng: eng.tensor_tensor(out=c2[:, :W], in0=c1[:, :W], in1=c2[:, :W],
                                            op=ALU.add))
            V(lambda eng: eng.tensor_scalar(out=c1[:, :W], in0=tmp[:, :W], scalar1=0.0,
                                            scalar2=None, op0=ALU.is_gt))
            V(lambda eng: eng.tensor_scalar(out=c1[:, :W], in0=c1[:, :W], scalar1=2.0,
                                            scalar2=-1.0, op0=ALU.mult, op1=ALU.add))
            first = True
            for s_ in qs["tie_up"]:
                if first:
                    V(lambda eng, s_=s_: eng.tensor_scalar(
                        out=mu[:, :W], in0=av[:, :W], scalar1=float(-s_), scalar2=None,
                        op0=ALU.is_equal))
                    first = False
                else:
                    V(lambda eng, s_=s_: eng.scalar_tensor_tensor(
                        out=mu[:, :W], in0=av[:, :W], scalar=float(-s_), in1=mu[:, :W],
                        op0=ALU.is_equal, op1=ALU.add))
            if first:
                V(lambda eng: eng.memset(mu[:, :W], 0.0))
            first = True
            for s_ in qs["tie_even"]:
                if first:
                    V(lambda eng, s_=s_: eng.tensor_scalar(
                        out=me[:, :W], in0=av[:, :W], scalar1=float(-s_), scalar2=None,
                        op0=ALU.is_equal))
                    first = False
                else:
                    V(lambda eng, s_=s_: eng.scalar_tensor_tensor(
                        out=me[:, :W], in0=av[:, :W], scalar=float(-s_), in1=me[:, :W],
                        op0=ALU.is_equal, op1=ALU.add))
            if first:
                V(lambda eng: eng.memset(me[:, :W], 0.0))
            V(lambda eng: eng.tensor_scalar(out=av[:, :W], in0=mu[:, :W], scalar1=-1.0,
                                            scalar2=1.0, op0=ALU.mult, op1=ALU.add))
            V(lambda eng: eng.tensor_tensor(out=av[:, :W], in0=av[:, :W], in1=me[:, :W],
                                            op=ALU.subtract))
            V(lambda eng: eng.scalar_tensor_tensor(out=me[:, :W], in0=c1[:, :W], scalar=0.5,
                                                   in1=dv[:, :W], op0=ALU.mult, op1=ALU.add))
            V(lambda eng: eng.scalar_tensor_tensor(out=e1b[:, :W], in0=c1[:, :W], scalar=-0.5,
                                                   in1=dv[:, :W], op0=ALU.mult, op1=ALU.add))
            V(lambda eng: eng.tensor_tensor(out=mu[:, :W], in0=mu[:, :W], in1=me[:, :W],
                                            op=ALU.mult))
            V(lambda eng: eng.tensor_tensor(out=av[:, :W], in0=av[:, :W], in1=e1b[:, :W],
                                            op=ALU.mult))
            V(lambda eng: eng.tensor_tensor(out=mu[:, :W], in0=mu[:, :W], in1=av[:, :W],
                                            op=ALU.add))
            V(lambda eng: eng.tensor_tensor(out=mu[:, :W], in0=mu[:, :W], in1=c2[:, :W],
                                            op=ALU.mult))
            V(lambda eng: eng.tensor_tensor(out=r0[:, :W], in0=r0[:, :W], in1=mu[:, :W],
                                            op=ALU.add))
        p.op("dve", lambda eng: eng.tensor_tensor(out=out_ap, in0=r0[:, :W], in1=e1[:, :W],
                                                  op=ALU.mult),
             reads=[ks], writes=list(writes))

    def emit_quant_sim(x_ap, out_ap, nconst, lo_clamp, reads, writes):
        W = x_ap.shape[1]
        for o in range(0, W, QW):
            w = min(QW, W - o)
            emit_quant_sim_blk(x_ap[:, o:o + w], out_ap[:, o:o + w], nconst, lo_clamp,
                               reads, writes)

    def quantize(x_ap, out_ap, kind, reads, writes):
        if not _SIM:
            p.op("dve", lambda eng: eng.tensor_copy(out=out_ap, in_=x_ap),
                 reads=list(reads), writes=list(writes))
        elif kind == "fq":
            emit_quant_sim(x_ap, out_ap, 2, -14.0, reads, writes)
        else:
            emit_quant_sim(x_ap, out_ap, 7, -50.0, reads, writes)

    p.op("dve", lambda eng: eng.memset(consts[:, 0:1], 1e-38), writes=["consts"])
    p.op("dve", lambda eng: eng.memset(consts[:, 1:2], 1e-5), writes=["consts"])
    p.op("dve", lambda eng: eng.memset(onesb[:, 0:1], 1.0), writes=["onesb"])
    p.op("dve", lambda eng: eng.memset(ones1f[:, :], 1.0), writes=["ones1f"])
    p.op("dve", lambda eng: eng.memset(onesf[:, 0:1], 1.0), writes=["onesf"])
    p.op("dve", lambda eng: eng.memset(ones1b[:, :], 1.0), writes=["ones1b"])
    p.op("sp", lambda eng: eng.dma_start(out=trix[:], in_=d_trix), writes=["trix"], queue="sp")
    p.op("sp", lambda eng: eng.dma_start(out=b1q[:, :], in_=d_b1c), writes=["b1f"], queue="sp")
    p.op("sp", lambda eng: eng.dma_start(out=b2q[:, :], in_=d_b2c), writes=["b2f"], queue="sp")
    quantize(b1q[:, :], b1q8[:, :], "fq", ["b1f"], ["b1q8"])
    quantize(b2q[:, :], b2q8[:, :], "fq", ["b2f"], ["b2q8"])
    p.op("dve", lambda eng: eng.tensor_copy(out=b1q[:, :], in_=b1q8[:, :]),
         reads=["b1q8"], writes=["b1q"])
    p.op("dve", lambda eng: eng.tensor_copy(out=b2q[:, :], in_=b2q8[:, :]),
         reads=["b2q8"], writes=["b2q"])

    keff = SBA("keff", 8 * L, BF16)
    vsb = SBR("vsb", 4 * C, BF16)
    wuqT_b = SBR("wuqT_b", 4096, BF16)
    wdkvT_b = SBR("wdkvT_b", 4096, BF16)
    wdqT_b = SBR("wdqT_b", 4096, BF16)
    stg = SBR("stg", 2 * 4096, F32)
    mt = SBR("mt", 8 * C, BF16)
    wdq_b = SBR("wdq_b", 4096, BF16)
    wuk_b = SBR("wuk_b", 4096, BF16)
    wuv_b = SBR("wuv_b", 4096, BF16)
    woT_b = SBR("woT_b", 8192, BF16)

    def load_cast(dst, dst_key, src_ap, cols, half):
        a = src_ap.shape[0] // 128
        w = cols // a
        sslice = stg[:, half * 4096: half * 4096 + cols]
        skey = f"stg{half}"
        p.op("sp", lambda eng: eng.dma_start(
            out=sslice.rearrange("p (a n) -> p a n", a=a),
            in_=src_ap.rearrange("(a p) n -> p a n", p=128)),
            writes=[skey], queue="sp")
        p.op("dve", lambda eng: eng.tensor_copy(out=dst[:, 0:cols], in_=sslice),
             reads=[skey], writes=[dst_key])

    load_cast(wdqT_b, "wdqT_b", d_wdqT, 4096, 0)
    load_cast(wuqT_b, "wuqT_b", d_wuqT, 4096, 1)
    load_cast(wdq_b, "wdq_b", d_wdq, 4096, 0)
    load_cast(wuk_b, "wuk_b", d_wuk, 4096, 1)
    load_cast(wuv_b, "wuv_b", d_wuv, 4096, 0)
    load_cast(wdkvT_b, "wdkvT_b", d_wdkvT, 4096, 1)
    for h_ in range(2):
        sslice = stg[:, h_ * 4096:(h_ + 1) * 4096]
        skey = f"stg{h_}"
        p.op("sp", lambda eng, s=sslice, h_=h_: eng.dma_start(
            out=s.rearrange("p (a n) -> p a n", n=1024),
            in_=d_woT.rearrange("(b a p) n -> b p a n", b=2, p=128)[h_]),
            writes=[skey], queue="sp")
        p.op("dve", lambda eng, s=sslice, h_=h_: eng.tensor_copy(
            out=woT_b[:, h_ * 4096:(h_ + 1) * 4096], in_=s),
            reads=[skey], writes=["woT_b"])

    def mm_block(dst_ap, dst_key, pairs, read_keys):
        pk, pt = ppool.take()
        nk = len(pairs)
        for i, (la, ra) in enumerate(pairs):
            p.op("pe", lambda eng, la=la, ra=ra, i=i, pt=pt, nk=nk: eng.matmul(
                pt[:, 0:ra.shape[1]], la, ra, start=(i == 0), stop=(i == nk - 1)),
                reads=list(read_keys), writes=[pk])
        p.op("act", lambda eng, pt=pt, o=dst_ap: eng.activation(
            out=o, in_=pt[:, 0:o.shape[1]], func=AF.Copy),
            reads=[pk], writes=[dst_key])

    for jc in range(8):
        for ib in range(2):
            pairs = [(wuqT_b[:, qc * 1024 + jc * 128: qc * 1024 + (jc + 1) * 128],
                      wdq_b[:, qc * 1024 + ib * 512: qc * 1024 + (ib + 1) * 512])
                     for qc in range(4)]
            mm_block(mt[:, jc * 1024 + ib * 512: jc * 1024 + (ib + 1) * 512], "mt",
                     pairs, ["wuqT_b", "wdq_b"])
    for ic in range(8):
        pairs = [(mt[:, jc * 1024 + ic * 128: jc * 1024 + (ic + 1) * 128],
                  wuk_b[:, jc * 512:(jc + 1) * 512]) for jc in range(8)]
        mm_block(keff[:, ic * 512:(ic + 1) * 512], "keff", pairs, ["mt", "wuk_b"])
    for lc in range(4):
        for cb in range(2):
            pairs = [(wuv_b[:, cc * 512 + lc * 128: cc * 512 + (lc + 1) * 128],
                      woT_b[:, cc * 1024 + cb * 512: cc * 1024 + (cb + 1) * 512])
                     for cc in range(8)]
            mm_block(vsb[:, lc * 1024 + cb * 512: lc * 1024 + (cb + 1) * 512], "vsb",
                     pairs, ["wuv_b", "woT_b"])
    SBFREE("woT_b", "wuv_b", "wuk_b", "wdq_b", "mt", "stg")

    xnb = SBR("xnb", 8 * T, BF16)
    xf = SBR("xf", (T if _SIM else 2 * T), F32)
    xnf = SBR("xnf", T, F32)
    rbsb = SBR("rbsb", T, F32)
    mrsb = SBR("mrsb", T, F32)
    rowV = SBR("rowV", T, F32, parts=1)
    rowT = SBR("rowT", T, F32, parts=1)
    for nk in ("xnb", "xf0", "xf1", "xbq", "x2q", "xnf", "rbsb", "mrsb", "lnvar", "lntmp"):
        p.alias(nk, ["mt", "wdq_b", "wuk_b", "wuv_b", "woT_b", "stg0", "stg1"])

    xT_r = d_xT.rearrange("(a p) t -> a p t", p=128)
    for cj in range(8):
        _sl = 0 if _SIM else (cj % 2)
        xf_sl = xf[:, _sl * T:(_sl + 1) * T]
        xkey = f"xf{_sl}"
        p.op("pool", lambda eng, cj=cj, s=xf_sl: eng.dma_start(out=s, in_=xT_r[cj]),
             writes=[xkey], queue="pool")
        p.op("act", lambda eng, s=xf_sl: eng.activation(out=xnf[:, :], in_=s, func=AF.Square),
             reads=[xkey], writes=["xsq"])
        for b in range(NB):
            p.op("pe", lambda eng, b=b, cj=cj, s=xf_sl: eng.matmul(
                psA[0:1, b * 512:(b + 1) * 512], onesf[:, 0:1],
                s[:, b * 512:(b + 1) * 512], start=(cj == 0), stop=(cj == 7)),
                reads=[xkey, "onesf"], writes=[f"psA{b}"])
            p.op("pe", lambda eng, b=b, cj=cj: eng.matmul(
                pool_tiles[b][1][0:1, :], onesf[:, 0:1],
                xnf[:, b * 512:(b + 1) * 512], start=(cj == 0), stop=(cj == 7)),
                reads=["xsq", "onesf"], writes=[pool_tiles[b][0]])
    p.op("dve", lambda eng: eng.tensor_scalar(out=rowM[:, :], in0=psA[0:1, :], scalar1=1.0 / C,
                                              scalar2=None, op0=ALU.mult),
         reads=psA_keys, writes=["lnmean"])
    for b in range(NB):
        p.op("dve", lambda eng, b=b: eng.tensor_scalar(
            out=rowV[:, b * 512:(b + 1) * 512], in0=pool_tiles[b][1][0:1, :],
            scalar1=1.0 / C, scalar2=None, op0=ALU.mult),
            reads=[pool_tiles[b][0]], writes=["lnvar"])
    p.op("act", lambda eng: eng.activation(out=rowT[:, :], in_=rowM[:, :], func=AF.Square),
         reads=["lnmean"], writes=["lntmp"])
    p.op("dve", lambda eng: eng.tensor_tensor(out=rowV[:, :], in0=rowV[:, :], in1=rowT[:, :],
                                              op=ALU.subtract),
         reads=["lnvar", "lntmp"], writes=["lnvar"])
    p.op("act", lambda eng: eng.activation(out=rowV[:, :], in_=rowV[:, :], func=AF.Ln,
                                           bias=consts[0:1, 1:2], scale=1.0),
         reads=["lnvar", "consts"], writes=["lnvar"])
    p.op("act", lambda eng: eng.activation(out=rowV[:, :], in_=rowV[:, :], func=AF.Exp,
                                           scale=-0.5),
         reads=["lnvar"], writes=["lnvar"])
    p.op("dve", lambda eng: eng.tensor_tensor(out=rowT[:, :], in0=rowM[:, :], in1=rowV[:, :],
                                              op=ALU.mult),
         reads=["lnmean", "lnvar"], writes=["lntmp"])
    for b in range(NB):
        pk, pt = ppool.take()
        p.op("pe", lambda eng, b=b, pt=pt: eng.matmul(
            pt[:, :], ones1f[:, :], rowV[:, b * 512:(b + 1) * 512], start=True, stop=True),
            reads=["lnvar", "ones1f"], writes=[pk])
        p.op("dve", lambda eng, b=b, pt=pt: eng.tensor_copy(
            out=rbsb[:, b * 512:(b + 1) * 512], in_=pt[:, :]),
            reads=[pk], writes=["rbsb"])
    for b in range(NB):
        pk2, pt2 = ppool.take()
        p.op("pe", lambda eng, b=b, pt=pt2: eng.matmul(
            pt[:, :], ones1f[:, :], rowT[:, b * 512:(b + 1) * 512], start=True, stop=True),
            reads=["lntmp", "ones1f"], writes=[pk2])
        p.op("dve", lambda eng, b=b, pt=pt2: eng.tensor_copy(
            out=mrsb[:, b * 512:(b + 1) * 512], in_=pt[:, :]),
            reads=[pk2], writes=["mrsb"])
    SBFREE("rowT", "rowV")

    for cj in range(8):
        _sl = 0 if _SIM else (cj % 2)
        xf_sl = xf[:, _sl * T:(_sl + 1) * T]
        xkey = f"xf{_sl}"
        p.op("pool", lambda eng, cj=cj, s=xf_sl: eng.dma_start(out=s, in_=xT_r[cj]),
             writes=[xkey], queue="pool")
        p.op("dve", lambda eng, s=xf_sl: eng.tensor_tensor(out=xnf[:, :], in0=s, in1=rbsb[:, :],
                                                           op=ALU.mult),
             reads=[xkey, "rbsb"], writes=["xnf"])
        p.op("dve", lambda eng: eng.tensor_tensor(out=xnf[:, :], in0=xnf[:, :], in1=mrsb[:, :],
                                                  op=ALU.subtract),
             reads=["xnf", "mrsb"], writes=["xnf"])
        p.op("act", lambda eng, cj=cj: eng.activation(out=xnb[:, cj * T:(cj + 1) * T],
                                                      in_=xnf[:, :], func=AF.Copy),
             reads=["xnf"], writes=["xnb"])
        quantize(xnf[:, :], xn_q[:, cj * T:(cj + 1) * T], "fq", ["xnf"], ["xn_q"])
    dump("g_xnb", xnb[:], ["xnb"])
    dump("g_xnq", xn_q[:], ["xn_q"])
    dump("g_rbsb", rbsb[:], ["rbsb"])
    dump("g_mrsb", mrsb[:], ["mrsb"])
    dump("g_rowM", rowM[:], ["lnmean"])
    SBFREE("mrsb", "rbsb", "xnf", "xf")

    q1b = SBR("q1b", 4 * T, BF16)
    p.alias("q1b", ["xf0", "xf1", "xbq", "x2q", "xnf", "rbsb", "mrsb"])
    for qs_ in range(4):
        for b in range(NB):
            pairs = [(wdqT_b[:, cc * 512 + qs_ * 128: cc * 512 + (qs_ + 1) * 128],
                      xnb[:, cc * T + b * 512: cc * T + (b + 1) * 512]) for cc in range(8)]
            mm_block(q1b[:, qs_ * T + b * 512: qs_ * T + (b + 1) * 512], "q1b",
                     pairs, ["wdqT_b", "xnb"])
    qT = SBA("qT", 8 * T, BF16)
    for cs in range(8):
        for b in range(NB):
            pairs = [(wuqT_b[:, qc * 1024 + cs * 128: qc * 1024 + (cs + 1) * 128],
                      q1b[:, qc * T + b * 512: qc * T + (b + 1) * 512]) for qc in range(4)]
            mm_block(qT[:, cs * T + b * 512: cs * T + (b + 1) * 512], "qT",
                     pairs, ["wuqT_b", "q1b"])
    dump("g_qT", qT[:], ["qT"])
    SBFREE("q1b")
    ckvT = SBA("ckvT", 4 * T, BF16)
    for ls in range(4):
        for b in range(NB):
            pairs = [(wdkvT_b[:, cc * 512 + ls * 128: cc * 512 + (ls + 1) * 128],
                      xnb[:, cc * T + b * 512: cc * T + (b + 1) * 512]) for cc in range(8)]
            mm_block(ckvT[:, ls * T + b * 512: ls * T + (b + 1) * 512], "ckvT",
                     pairs, ["wdkvT_b", "xnb"])
    dump("g_ckvT", ckvT[:], ["ckvT"])
    dump("g_keff", keff[:], ["keff"])
    dump("g_vsb", vsb[:], ["vsb"])
    SBFREE("xnb", "wdqT_b", "wdkvT_b", "wuqT_b")
    usb = SBA("usb", 16 * (NH * 65), BF16)
    for _kc in range(16):
        for _hh in range(NH):
            p.op("dve", lambda eng, _kc=_kc, _hh=_hh: eng.memset(
                usb[:, _kc * 1040 + _hh * 65 + 64: _kc * 1040 + _hh * 65 + 65], 1.0),
                writes=["usb_ones"])
    for kc in range(16):
        for cb in range(2):
            pk, pt = ppool.take()
            for lc in range(4):
                p.op("pe", lambda eng, kc=kc, cb=cb, lc=lc, pt=pt: eng.matmul(
                    pt[:, :], ckvT[:, lc * T + kc * 128: lc * T + (kc + 1) * 128],
                    vsb[:, lc * 1024 + cb * 512: lc * 1024 + (cb + 1) * 512],
                    start=(lc == 0), stop=(lc == 3)),
                    reads=["ckvT", "vsb"], writes=[pk])
            out_ap = usb[:, kc * 1040 + cb * 8 * 65: kc * 1040 + (cb * 8 + 8) * 65] \
                .rearrange("p (h o) -> p h o", h=8)[:, :, 0:64]
            p.op("act", lambda eng, pt=pt, o=out_ap: eng.activation(
                out=o, in_=pt[:, :].rearrange("p (h d) -> p h d", h=8), func=AF.Copy),
                reads=[pk], writes=["usb"])
    dump("g_usb", usb[:], ["usb", "usb_ones"])
    SBFREE("vsb")

    stg2 = SBR("stg2", 4096, F32)
    stg8 = SBR("stg8", 4096, FP8)
    for nk in ("stg2_0", "stg8_0"):
        p.alias(nk, ["q1b", "xnb", "wdqT_b", "wdkvT_b", "wuqT_b"])
    w1T_r = d_w1T.rearrange("(a p) n -> a p n", p=128)
    w1q8_r = d_w1q8.rearrange("(a p) n -> a p n", p=128)
    w2T_r = d_w2T.rearrange("(a p) n -> a p n", p=128)
    w2q8_r = d_w2q8.rearrange("(a p) n -> a p n", p=128)
    wf_steps = []

    def _wf_step(mat_r, q8_r, a, cols):
        def go():
            sslice = stg2[:, 0:cols]
            skey = "stg2_0"
            s8 = stg8[:, 0:cols]
            s8key = "stg8_0"
            dkey = (f"w1q8dram_{a}" if q8_r is w1q8_r else f"w2q8dram_{a}")
            p.op("sp", lambda eng, a=a, s=sslice: eng.dma_start(out=s, in_=mat_r[a]),
                 writes=[skey], queue="sp")
            quantize(sslice, s8, "fq", [skey], [s8key])
            p.op("pool", lambda eng, a=a, s8=s8: eng.dma_start(out=q8_r[a], in_=s8),
                 reads=[s8key], writes=[dkey], queue="pool")
        return go

    for a in range(8):
        wf_steps.append(_wf_step(w1T_r, w1q8_r, a, 4096))
    for a in range(32):
        wf_steps.append(_wf_step(w2T_r, w2q8_r, a, 1024))

    qlat = SBA("qlat", 4 * T, BF16)
    ptb = SBA("ptb", 2 * T, BF16)
    yae = SBA("yae", 1024, F32)
    ynb = SBA("ynb", T, BF16)
    ss32 = SBA("ss32", 32, F32, parts=32)
    rrow = SBA("rrow", 1024, F32, parts=1)
    for nk in ("qlat", "pt0_0", "pt0_1", "pt0_2", "pt0_3", "pt1_0", "pt1_1", "pt1_2",
               "pt1_3", "yae0", "yae1", "ss32", "rrow"):
        p.alias(nk, ["stg2_0", "stg2_1", "stg8_0", "stg8_1", "vsb"])

    def emit_qlat(h):
        b0 = (h % 2) * 64
        hc = h // 2
        for b in range(NB):
            for lc in range(4):
                pk, pt = ppool.take()
                p.op("pe", lambda eng, lc=lc, b=b, pt=pt, b0=b0, hc=hc: eng.matmul(
                    pt[:, :],
                    keff[b0:b0 + 64, hc * 512 + lc * 128: hc * 512 + (lc + 1) * 128],
                    qT[b0:b0 + 64, hc * T + b * 512: hc * T + (b + 1) * 512],
                    start=True, stop=True),
                    reads=["keff", "qT"], writes=[pk])
                p.op("act", lambda eng, lc=lc, b=b, pt=pt: eng.activation(
                    out=qlat[:, lc * T + b * 512: lc * T + (b + 1) * 512], in_=pt[:, :],
                    func=AF.Copy),
                    reads=[pk], writes=[f"qlat_{b}"])

    def emit_qk(h, kt):
        ptile = ptb[:, (kt % 2) * T:(kt % 2 + 1) * T]
        ptkey = f"pt{kt % 2}"
        b_lo = kt // 4
        for b in range(b_lo, NB):
            pk, pt = ppool.take()
            for lc in range(4):
                p.op("pe", lambda eng, kt=kt, b=b, lc=lc, pt=pt: eng.matmul(
                    pt[:, :], ckvT[:, lc * T + kt * 128: lc * T + (kt + 1) * 128],
                    qlat[:, lc * T + b * 512: lc * T + (b + 1) * 512],
                    start=(lc == 0), stop=(lc == 3)),
                    reads=["ckvT", f"qlat_{b}"], writes=[pk])
            p.op("act", lambda eng, b=b, pt=pt, ptile=ptile: eng.activation(
                out=ptile[:, b * 512:(b + 1) * 512], in_=pt[:, :], func=AF.Exp, scale=0.125),
                reads=[pk], writes=[ptkey + f"_{b}"])
            if b == b_lo:
                o = 128 * (kt % 4)
                p.op("dve", lambda eng, b=b, ptile=ptile, o=o: eng.tensor_tensor(
                    out=ptile[:, b * 512:(b + 1) * 512], in0=ptile[:, b * 512:(b + 1) * 512],
                    in1=trix[:, 384 - o: 896 - o], op=ALU.mult),
                    reads=[ptkey + f"_{b}", "trix"], writes=[ptkey + f"_{b}"])

    def emit_yatt(h, kt):
        ptile = ptb[:, (kt % 2) * T:(kt % 2 + 1) * T]
        ptkey = f"pt{kt % 2}"
        b_lo = kt // 4
        for b in range(b_lo, NB):
            p.op("pe", lambda eng, kt=kt, b=b, h=h, ptile=ptile: eng.matmul(
                psA[0:65, b * 512:(b + 1) * 512],
                usb[:, kt * 1040 + h * 65: kt * 1040 + (h + 1) * 65],
                ptile[:, b * 512:(b + 1) * 512],
                start=(kt == 0), stop=(kt == min(15, 4 * b + 3))),
                reads=[ptkey + f"_{b}", "usb", "usb_ones"], writes=[f"psA{b}"])

    def emit_epi_block(h, b):
        if h == 0 and b == 0:
            for bb in range(NB):
                p.alias(f"sumsrow_{bb}", ["lnmean"])
        p.op("act", lambda eng, b=b: eng.activation(
            out=rowM[:, b * 512:(b + 1) * 512], in_=psA[64:65, b * 512:(b + 1) * 512],
            func=AF.Copy),
            reads=[f"psA{b}"], writes=[f"sumsrow_{b}"])
        ya_sl = yae[:, (b % 2) * 512:(b % 2 + 1) * 512]
        p.op("act", lambda eng, b=b, ya=ya_sl: eng.activation(
            out=ya[0:64, :], in_=psA[0:64, b * 512:(b + 1) * 512], func=AF.Copy),
            reads=[f"psA{b}"], writes=[f"yae{b % 2}"])

    def emit_epi_recip(h, pair):
        blo = 2 * pair
        p.op("pool", lambda eng, blo=blo: eng.dma_start(
            out=ss32[:, :], in_=rowM[0:1, blo * 512:(blo + 2) * 512]),
            reads=[f"sumsrow_{blo}", f"sumsrow_{blo + 1}"], writes=["ss32"], queue="pool")
        p.op("dve", lambda eng: eng.reciprocal(out=ss32[:, :], in_=ss32[:, :]),
             reads=["ss32"], writes=["ss32"])
        p.op("pool", lambda eng: eng.dma_start(
            out=rrow[0:1, :], in_=ss32[:, :]),
            reads=["ss32"], writes=["rrow"], queue="pool")

    def emit_epi_norm(h, pair):
        b0 = (h % 2) * 64
        hc = h // 2
        for b in (2 * pair, 2 * pair + 1):
            pk, pt = ppool.take()
            p.op("pe", lambda eng, b=b, pt=pt: eng.matmul(
                pt[0:64, :], ones1f[:, 0:64], rrow[:, (b % 2) * 512:(b % 2 + 1) * 512],
                start=True, stop=True),
                reads=["rrow", "ones1f"], writes=[pk])
            ya_sl = yae[:, (b % 2) * 512:(b % 2 + 1) * 512]
            if b0 == 0:
                p.op("dve", lambda eng, b=b, pt=pt, ya=ya_sl, hc=hc: eng.tensor_tensor(
                    out=yatt[0:64, hc * T + b * 512: hc * T + (b + 1) * 512],
                    in0=ya[0:64, :], in1=pt[0:64, :], op=ALU.mult),
                    reads=[f"yae{b % 2}", pk], writes=["yatt"])
            else:
                p.op("dve", lambda eng, b=b, pt=pt, ya=ya_sl: eng.tensor_tensor(
                    out=ynb[0:64, b * 512:(b + 1) * 512],
                    in0=ya[0:64, :], in1=pt[0:64, :], op=ALU.mult),
                    reads=[f"yae{b % 2}", pk], writes=["ynb"])
        if b0 == 64 and pair == 1:
            p.op("pool", lambda eng, hc=hc: eng.dma_start(
                out=yatt[64:128, hc * T:(hc + 1) * T], in_=ynb[0:64, :]),
                reads=["ynb"], writes=["yatt"], queue="pool")

    wfi = 0
    tail_prev = None
    for h in range(NH):
        emit_qlat(h)
        if h == 0:
            dump("g_qlat", qlat[:], [f"qlat_{b}" for b in range(NB)])
        prev = None
        for kt in range(16):
            emit_qk(h, kt)
            if h == 0 and kt == 1:
                dump("g_pt", ptb[:], ["pt0_0", "pt0_1", "pt0_2", "pt0_3",
                                      "pt1_0", "pt1_1", "pt1_2", "pt1_3"])
            if kt == 1 and tail_prev is not None:
                tail_prev()
                tail_prev = None
            if prev is not None:
                emit_yatt(h, prev)
                if prev % 4 == 3 and prev < 15:
                    emit_epi_block(h, prev // 4)
            if kt == 9:
                emit_epi_recip(h, 0)
            if kt == 11:
                emit_epi_norm(h, 0)
            prev = kt
        emit_yatt(h, prev)
        emit_epi_block(h, 3)
        emit_epi_recip(h, 1)
        tail_prev = (lambda h=h: emit_epi_norm(h, 1))
        nsteps = (len(wf_steps) * min(h + 1, NH - 2)) // (NH - 2) - wfi
        for _ in range(nsteps):
            wf_steps[wfi]()
            wfi += 1
    tail_prev()
    assert wfi == len(wf_steps)
    dump("g_yatt", yatt[:], ["yatt"])
    SBFREE("stg8", "stg2", barrier=False)

    SBFREE("rrow", "ss32", "ynb", "yae", "ptb", "qlat", "usb", "ckvT", "qT", "keff",
           barrier=False)
    w1q = SBA("w1q", 8 * N, FP8)
    w2q = SBA("w2q", 32 * C, FP8)
    hq = SBA("hq", 32 * 512, FP8)
    hbf = SBA("hbf", 4 * 512, BF16)
    mstg = SBA("mstg", 4096, F32)
    xstg = SBA("xstg", 4096, F32)
    ostg = SBA("ostg", 4096, F32)
    otmp = SBA("otmp", 512, F32)
    y2bf = SBA("y2bf", 2 * 512, BF16)
    if _SIM:
        h1f = SBA("h1f", 512, F32)
        y2f = SBA("y2f", 512, F32)
    qlat_keys = [f"qlat_{b}" for b in range(NB)]
    pt_keys = ["pt0_0", "pt0_1", "pt0_2", "pt0_3", "pt1_0", "pt1_1", "pt1_2", "pt1_3"]
    usb_keys = ["usb", "usb_ones"]
    old_keys = ["xnb", "xf0", "xf1", "xnf", "xsq", "rbsb", "mrsb", "lnvar", "lntmp",
                "q1b", "vsb", "stg0", "stg1", "stg2_0", "stg8_0"]
    for a in range(8):
        p.alias(f"w1q_{a}", ["keff", "qT"])
    for a in range(32):
        ks = (["qT"] if a <= 8 else []) + (["ckvT"] if 7 <= a <= 24 else []) \
            + (usb_keys if a >= 23 else [])
        p.alias(f"w2q_{a}", ks)
    p.alias("hq", usb_keys)
    for _hb in range(4):
        p.alias(f"hbf{_hb}", usb_keys)
    p.alias("y2bf0", ["ynb", "ss32", "rrow"] + old_keys)
    p.alias("y2bf1", ["ynb", "ss32", "rrow"] + old_keys)

    class _PsaView:
        def __init__(self, c0):
            self.c0 = c0

        def __getitem__(self, idx):
            r, c = idx
            start = (c.start or 0) + self.c0
            stop = (c.stop if c.stop is not None else 512) + self.c0
            return psA[r, start:stop]

    ppool8 = PsumPool(list(ppool.tiles) + [(f"psA{i}", _PsaView(i * 512))
                                           for i in range(4)])
    p.alias("mstg", usb_keys + qlat_keys)
    p.alias("xstg", qlat_keys + pt_keys + ["yae0", "yae1"])
    for cs in range(8):
        p.alias(f"ostg{cs}", ["yae0", "yae1", "ynb", "ss32", "rrow"] + old_keys)
    for nk in ("otmp", "y2bf", "h1f", "y2f"):
        p.alias(nk, ["ynb", "ss32", "rrow"] + old_keys)

    for a in range(8):
        p.op("sp", lambda eng, a=a: eng.dma_start(
            out=w1q[:, a * N:(a + 1) * N], in_=w1q8_r[a]),
            reads=[f"w1q8dram_{a}"], writes=[f"w1q_{a}"], queue="sp")
    for a in range(32):
        q = "sp" if a % 2 == 0 else "pool"
        p.op(q, lambda eng, a=a: eng.dma_start(
            out=w2q[:, a * C:(a + 1) * C], in_=w2q8_r[a]),
            reads=[f"w2q8dram_{a}"], writes=[f"w2q_{a}"], queue=q)

    mask_r = d_maskT.rearrange("(a p) (q t) -> q p a t", p=128, t=512)
    x_r2 = d_xT.rearrange("(a p) (q t) -> q p a t", p=128, t=512)
    out_r = d_outT.rearrange("(a p) (q t) -> q p a t", p=128, t=512)

    for ti in range(NB):
        p.op("sp", lambda eng, ti=ti: eng.dma_start(
            out=mstg[:].rearrange("p (a t) -> p a t", t=512), in_=mask_r[ti]),
            writes=["mstg"], queue="sp")
        p.op("pool", lambda eng, ti=ti: eng.dma_start(
            out=xstg[:].rearrange("p (a t) -> p a t", t=512), in_=x_r2[ti]),
            writes=["xstg"], queue="pool")
        for nj in range(32):
            pk, pt = ppool8.take()
            for cc in range(8):
                p.op("pe", lambda eng, nj=nj, cc=cc, ti=ti, pt=pt: eng.matmul(
                    pt[:, :], w1q[:, cc * N + nj * 128: cc * N + (nj + 1) * 128],
                    xn_q[:, cc * T + ti * 512: cc * T + (ti + 1) * 512],
                    start=(cc == 0), stop=(cc == 7)),
                    reads=[f"w1q_{cc}", "xn_q"], writes=[pk])
            hbf_sl = hbf[:, (nj % 4) * 512:(nj % 4 + 1) * 512]
            hbkey = f"hbf{nj % 4}"
            if not _SIM:
                p.op("act", lambda eng, nj=nj, pt=pt, o=hbf_sl: eng.activation(
                    out=o, in_=pt[:, :], func=AF.Relu, bias=b1q[:, nj:nj + 1], scale=1.0),
                    reads=[pk, "b1q"], writes=[hbkey])
                p.op("dve", lambda eng, nj=nj, o=hbf_sl: eng.tensor_copy(
                    out=hq[:, nj * 512:(nj + 1) * 512], in_=o),
                    reads=[hbkey], writes=["hq"])
            else:
                p.op("act", lambda eng, nj=nj, pt=pt: eng.activation(
                    out=h1f[:, :], in_=pt[:, :], func=AF.Relu, bias=b1q[:, nj:nj + 1],
                    scale=1.0),
                    reads=[pk, "b1q"], writes=["h1f"])
                emit_quant_sim(h1f[:, :], hbf_sl, 7, -50.0, ["h1f"], [hbkey])
                emit_quant_sim(hbf_sl, hq[:, nj * 512:(nj + 1) * 512], 2, -14.0,
                               [hbkey], ["hq"])
        if ti == 3:
            dump("g_hq", hq[:], ["hq"])
        for cs in range(8):
            pk, pt = ppool8.take()
            for nj in range(32):
                p.op("pe", lambda eng, cs=cs, nj=nj, pt=pt: eng.matmul(
                    pt[:, :], w2q[:, nj * C + cs * 128: nj * C + (cs + 1) * 128],
                    hq[:, nj * 512:(nj + 1) * 512],
                    start=(nj == 0), stop=(nj == 31)),
                    reads=[f"w2q_{nj}", "hq"], writes=[pk])
            y2_sl = y2bf[:, (cs % 2) * 512:(cs % 2 + 1) * 512]
            y2key = f"y2bf{cs % 2}"
            if not _SIM:
                p.op("dve", lambda eng, cs=cs, pt=pt, o=y2_sl: eng.tensor_scalar(
                    out=o, in0=pt[:, :], scalar1=b2q[:, cs:cs + 1], scalar2=None,
                    op0=ALU.add),
                    reads=[pk, "b2q"], writes=[y2key])
            else:
                p.op("dve", lambda eng, cs=cs, pt=pt: eng.tensor_scalar(
                    out=y2f[:, :], in0=pt[:, :], scalar1=b2q[:, cs:cs + 1], scalar2=None,
                    op0=ALU.add),
                    reads=[pk, "b2q"], writes=["y2f"])
                emit_quant_sim(y2f[:, :], y2_sl, 7, -50.0, ["y2f"], [y2key])
            p.op("dve", lambda eng, cs=cs, o=y2_sl: eng.tensor_tensor(
                out=otmp[:, :], in0=o, in1=mstg[:, cs * 512:(cs + 1) * 512],
                op=ALU.mult),
                reads=[y2key, "mstg"], writes=["otmp"])
            p.op("dve", lambda eng, cs=cs, ti=ti: eng.tensor_tensor(
                out=otmp[:, :], in0=otmp[:, :],
                in1=yatt[:, cs * T + ti * 512: cs * T + (ti + 1) * 512], op=ALU.add),
                reads=["otmp", "yatt"], writes=["otmp"])
            p.op("dve", lambda eng, cs=cs: eng.tensor_tensor(
                out=ostg[:, cs * 512:(cs + 1) * 512], in0=otmp[:, :],
                in1=xstg[:, cs * 512:(cs + 1) * 512], op=ALU.add),
                reads=["otmp", "xstg"], writes=[f"ostg{cs}"])
        p.op("pool", lambda eng, ti=ti: eng.dma_start(
            out=out_r[ti], in_=ostg[:].rearrange("p (a t) -> p a t", t=512)),
            reads=[f"ostg{cs_}" for cs_ in range(8)], writes=["outdram"], queue="pool")

    p.wait_only("pool", ["outdram"])
    p.wait_only("sp", ["outdram"])

    with nc.Block() as block:
        p.finish(block)
    es.close()
    return nc


def _host_layout(inputs):
    x = np.asarray(inputs["L0_attn_x"], dtype=np.float32)
    wdq = np.asarray(inputs["L0_attn_W_dq"], dtype=np.float32)
    wuq = np.asarray(inputs["L0_attn_W_uq"], dtype=np.float32)
    wdkv = np.asarray(inputs["L0_attn_W_dkv"], dtype=np.float32)
    wuk = np.asarray(inputs["L0_attn_W_uk"], dtype=np.float32)
    wuv = np.asarray(inputs["L0_attn_W_uv"], dtype=np.float32)
    wo = np.asarray(inputs["L0_attn_W_o"], dtype=np.float32)
    w1 = np.asarray(inputs["L0_ffn_W1"], dtype=np.float32)
    b1 = np.asarray(inputs["L0_ffn_b1"], dtype=np.float32)
    w2 = np.asarray(inputs["L0_ffn_W2"], dtype=np.float32)
    b2 = np.asarray(inputs["L0_ffn_b2"], dtype=np.float32)

    import jax
    cpu = jax.devices("cpu")[0]
    if _SIM:
        jax.config.update("jax_enable_x64", True)
    with jax.default_device(cpu):
        k42 = jax.random.key(42, impl="threefry2x32")
        keep = np.asarray(jax.random.bernoulli(k42, 0.8, (8, T, C)))
    maskv = np.where(keep, np.float32(1.25), np.float32(0.0)).astype(np.float32)

    kk = np.arange(128)[:, None]
    mm = np.arange(896)[None, :]
    trix = (mm >= kk + 384).astype(ml_dtypes.bfloat16)

    com = {
        "d_wdq": np.ascontiguousarray(wdq),
        "d_wdqT": np.ascontiguousarray(wdq.T),
        "d_wuqT": np.ascontiguousarray(wuq.T),
        "d_wuk": np.ascontiguousarray(wuk),
        "d_wuv": np.ascontiguousarray(wuv),
        "d_wdkvT": np.ascontiguousarray(wdkv.T),
        "d_woT": np.ascontiguousarray(wo.T),
        "d_w1T": np.ascontiguousarray(w1.T),
        "d_w2T": np.ascontiguousarray(w2.T),
        "d_b1c": np.ascontiguousarray(b1.reshape(32, 128).T),
        "d_b2c": np.ascontiguousarray(b2.reshape(8, 128).T),
        "d_trix": trix,
    }
    in_maps = []
    for core in range(8):
        m = dict(com)
        m["d_xT"] = np.ascontiguousarray(x[core].T)
        m["d_maskT"] = np.ascontiguousarray(maskv[core].T)
        in_maps.append(m)
    return in_maps


def kernel(**inputs):
    in_maps = _host_layout(inputs)
    nc = build_program()
    res = run_bass_kernel_spmd(nc, in_maps, core_ids=list(range(8)))
    out = np.stack([np.asarray(res.results[i]["d_outT"]) for i in range(8)], axis=0)
    return np.ascontiguousarray(out.transpose(0, 2, 1))
